# version 38
# baseline (speedup 1.0000x reference)
import numpy as np
from contextlib import ExitStack

import concourse.bass as bass
import concourse.mybir as mybir
from concourse.bass_utils import run_bass_kernel_spmd
from concourse.alu_op_type import AluOpType as ALU

F32 = mybir.dt.float32
BF16 = mybir.dt.bfloat16
AF = mybir.ActivationFunctionType
AX = mybir.AxisListType

ENGINES = ("sync", "gpsimd", "tensor", "vector", "scalar")

DEF_CFG = dict(D=2048, H=16, TP=1024, TM=1024, DK=64, DV=128, M=64, KE=8, R=64)
EPS = 1e-5
CLM, CLX = -32.0, 32.0
QSC = 0.125


class _Sem:
    __slots__ = ("h", "n")

    def __init__(self, h):
        self.h = h
        self.n = 0


class Prog:
    def __init__(self, nc):
        self.nc = nc
        self.streams = {e: [] for e in ENGINES}
        self.sems = {}
        self.hw = {e: {} for e in ENGINES}
        self.ninst = {e: 0 for e in ENGINES}

    def add_sem(self, name, handle):
        self.sems[name] = _Sem(handle)

    ONESHOT = ("SYI", "GPI")

    def wait(self, eng, tok):
        if tok is None:
            return
        sem, n = tok
        if sem in self.ONESHOT and n > 0:
            n = self.sems[sem].n
        if n <= 0 or self.hw[eng].get(sem, 0) >= n:
            return
        self.hw[eng][sem] = n
        h = self.sems[sem].h
        self.streams[eng].append(lambda e, h=h, n=n: e.wait_ge(h, n))
        self.ninst[eng] += 1

    def waits(self, eng, toks):
        for t in toks:
            self.wait(eng, t)

    def run(self, eng, fn):
        self.streams[eng].append(fn)
        self.ninst[eng] += 1

    def inc(self, eng, sem, fn, dn=1):
        s = self.sems[sem]
        s.n += dn
        h = s.h
        self.streams[eng].append(lambda e, fn=fn, h=h, dn=dn: fn(e).then_inc(h, dn))
        self.ninst[eng] += 1
        return (sem, s.n)

    def cur(self, sem):
        return (sem, self.sems[sem].n)


class Buf:
    __slots__ = ("h", "shape", "dtype", "birth", "nm", "pitch")

    def __init__(self, h, shape, dtype, birth, nm):
        self.h = h
        self.shape = list(shape)
        self.dtype = dtype
        self.birth = list(birth)
        self.nm = nm
        self.pitch = int(np.prod(shape[1:]))

    def __getitem__(self, sl):
        return self.h[sl]

    def ap(self, p0, np_, c0, nc_):
        return self.h[p0:p0 + np_, c0:c0 + nc_]

    def ap3(self, p0, np_, c0, pairs):
        return bass.AP(self.h, p0 * self.pitch + c0,
                       [[self.pitch, np_]] + [list(p) for p in pairs])


class Arena:

    def __init__(self, nc, nbytes):
        self.nc = nc
        self.nbytes = nbytes
        self._cm = nc.sbuf_tensor("arena", [128, nbytes], mybir.dt.uint8)
        self.handle = self._cm.__enter__()
        self.baseaddr = nc.lookup_mloc(self.handle).addr
        self.free = [[0, nbytes, [], 0]]
        self.live = {}
        self.idx = 0
        self.stamp = 0
        self.peak = 0

    def alloc(self, name, shape, dtype, toks=(), side="left"):
        per_part = int(np.prod(shape[1:])) * mybir.dt.size(dtype)
        per_part = (per_part + 31) // 32 * 32
        if side == "left" and per_part <= 16384:
            side = "right"
        cand = [g for g in self.free if g[1] - g[0] >= per_part]
        if side == "right":
            pick = [cand[-1]] if cand else []
        else:
            old = [g for g in cand if g[3] <= self.stamp - 6]
            pick = old if old else cand
        for seg in pick:
            if True:
                birth = list(seg[2]) + list(toks)
                if side == "right":
                    off = seg[1] - per_part
                    seg[1] -= per_part
                else:
                    off = seg[0]
                    seg[0] += per_part
                if seg[0] >= seg[1]:
                    self.free.remove(seg)
                self.idx += 1
                nm = f"{name}_{self.idx}"
                h = self.nc.alloc_sbuf_tensor_at(nm, list(shape), dtype,
                                                 offset=self.baseaddr + off)
                self.live[nm] = (off, per_part)
                inuse = self.nbytes - sum(s[1] - s[0] for s in self.free)
                self.peak = max(self.peak, inuse)
                return Buf(h, shape, dtype, birth, nm)
        live = sorted(((off, sz, nm) for nm, (off, sz) in self.live.items()))
        raise MemoryError(f"arena OOM for {name}: need {per_part}B/part, "
                          f"free={[(s[0], s[1]) for s in self.free]}, "
                          f"live={live}")

    def freebuf(self, buf, toks=()):
        off, sz = self.live.pop(buf.nm)
        self.stamp += 1
        self.free.append([off, off + sz, list(toks), self.stamp])
        self.free.sort(key=lambda s: s[0])
        merged = []
        for seg in self.free:
            if merged and merged[-1][1] == seg[0]:
                merged[-1][1] = seg[1]
                merged[-1][2] = merged[-1][2] + seg[2]
                merged[-1][3] = max(merged[-1][3], seg[3])
            else:
                merged.append(seg)
        self.free = merged


def build_program(cfg):
    D, H, TP, TM = cfg["D"], cfg["H"], cfg["TP"], cfg["TM"]
    DK, DV, M, KE, R = cfg["DK"], cfg["DV"], cfg["M"], cfg["KE"], cfg["R"]
    KC = D // 128
    VD = H * DV
    VDC = VD // 128
    QD = H * DK
    MD = H * M
    NPC = TP // 128
    TH = TM // 2
    NHC = TH // 128
    GH = 4 if H % 4 == 0 else H
    NG = H // GH
    KR = KE * R
    KRC = KR // 128
    D2 = D // 2
    GV = VD // 2
    HH = H // 2

    nc = bass.Bass("TRN2", target_bir_lowering=False)

    def dri(name, shape):
        return nc.dram_tensor(name, list(shape), F32, kind="ExternalInput").ap()

    x_main = dri("x_main", [TM, D])
    x_pre = dri("x_pre", [TP, D])
    wq = dri("wq", [D, QD]); wk = dri("wk", [D, QD]); ws = dri("ws", [D, MD])
    wv = dri("wv", [D, VD]); wg = dri("wg", [D, VD]); wo = dri("wo", [VD, D])
    w1 = dri("w1", [D, D]); w2 = dri("w2", [D, KE])
    ut = dri("ut", [KR, D]); v2t = dri("v2t", [D, KR])
    b1b = dri("b1b", [128, D]); b2b = dri("b2b", [128, KE])
    gnwb = dri("gnwb", [128, DV]); csts = dri("csts", [128, 8])
    pbias = dri("pbias", [128, 1])
    identf = dri("identf", [128, 128]); lmaskf = dri("lmaskf", [128, 128])
    onescol = dri("onescol", [128, 1]); onesrow = dri("onesrow", [1, 128])
    yout = nc.dram_tensor("y", [TM, D], F32, kind="ExternalOutput").ap()

    ctx = ExitStack()
    with ctx:
        FB = [ctx.enter_context(nc.psum_tensor(f"fb{i}", [128, 512], F32))
              for i in range(6)]
        TBK = [ctx.enter_context(nc.psum_tensor(f"tb{i}", [128, 1024], BF16))
               for i in range(2)]

        P = Prog(nc)
        for s in (("SYI", "GPI", "GPC", "MM", "VE", "SC", "ST0", "ST1",
                   "OD0", "OD1")
                  + tuple(f"W{i}" for i in range(8))):
            P.add_sem(s, ctx.enter_context(nc.semaphore(s)))
        wsem_rot = [0]

        def next_wsem():
            nm = f"W{wsem_rot[0] % 8}"
            wsem_rot[0] += 1
            return nm

        AR = Arena(nc, 206 * 1024)

        def ldma(out_ap, in_ap, waits=(), sem="SYI", quiesce=None):
            P.waits("sync", waits)
            if (quiesce is None and sem not in Prog.ONESHOT) or quiesce:
                P.wait("sync", (sem, P.sems[sem].n))
            return P.inc("sync", sem,
                         lambda e, o=out_ap, i=in_ap: e.dma_start(out=o, in_=i), 16)

        def cdma(out_ap, in_ap, waits=(), sem="GPI", quiesce=None):
            P.waits("gpsimd", waits)
            if (quiesce is None and sem not in Prog.ONESHOT) or quiesce:
                P.wait("gpsimd", (sem, P.sems[sem].n))
            return P.inc("gpsimd", sem,
                         lambda e, o=out_ap, i=in_ap: e.dma_start(out=o, in_=i), 16)

        def sdma(out_ap, in_ap, waits=(), sem="OD0", quiesce=None):
            P.waits("scalar", waits)
            if (quiesce is None and sem not in Prog.ONESHOT) or quiesce:
                P.wait("scalar", (sem, P.sems[sem].n))
            return P.inc("scalar", sem,
                         lambda e, o=out_ap, i=in_ap: e.dma_start(out=o, in_=i), 16)

        def mm(out_ap, lhsT, rhs, start, stop, waits=(), inc=False, tp=False):
            P.waits("tensor", waits)
            fn = (lambda t, o=out_ap, l=lhsT, r=rhs, a=start, b=stop, tpp=tp:
                  t.matmul(o, l, r, start=a, stop=b, is_transpose=(True if tpp else None)))
            if inc:
                return P.inc("tensor", "MM", fn)
            P.run("tensor", fn)
            return None

        class _Rec:

            def __init__(self):
                self.call = None

            def __getattr__(self, name):
                def f(*a, **k):
                    self.call = (name, a, k)
                return f

        def _eager(fn):
            r = _Rec()
            fn(r)
            name, a, k = r.call
            return lambda e, n=name, a=a, k=k: getattr(e, n)(*a, **k)

        def vop(fn, waits=()):
            P.waits("vector", waits)
            P.wait("vector", ("VE", P.sems["VE"].n))
            return P.inc("vector", "VE", _eager(fn))

        def sop(fn, waits=()):
            P.waits("scalar", waits)
            P.wait("scalar", ("SC", P.sems["SC"].n))
            return P.inc("scalar", "SC", _eager(fn))

        def gop(fn, waits=()):
            P.waits("gpsimd", waits)
            P.wait("gpsimd", ("GPC", P.sems["GPC"].n))
            return P.inc("gpsimd", "GPC", _eager(fn))

        bank_free = {}

        def bfree(bank):
            return bank_free.get(id(bank))

        def bset(bank, tok):
            bank_free[id(bank)] = tok

        csb = AR.alloc("csb", [128, 8], F32, side="right")
        pb_sb = AR.alloc("pb", [128, 1], F32, side="right")
        idf_sb = AR.alloc("idf", [128, 128], F32, side="right")
        idb_sb = AR.alloc("idb", [128, 128], BF16, side="right")
        lmf_sb = AR.alloc("lmf", [128, 128], F32, side="right")
        lmb_sb = AR.alloc("lmb", [128, 128], BF16, side="right")
        oc_sb = AR.alloc("ocol", [128, 1], BF16, side="right")
        or_sb = AR.alloc("orow", [1, 128], BF16, side="right")
        gnw_sb = AR.alloc("gnw", [128, DV], F32, side="right")
        b2_sb = AR.alloc("b2", [128, KE], F32, side="right")

        for v, i in ((0.0, 0), (1.0, 1), (EPS, 2), (1.0 / DV, 3)):
            nc.const_aps.register(dtype=F32, value=v, ap=csb.h[0:128, i:i + 1])

        t_cs = ldma(csb[:, :], csts)
        t_pb = ldma(pb_sb[:, :], pbias)
        t_idf = ldma(idf_sb[:, :], identf)
        t_lmf = ldma(lmf_sb[:, :], lmaskf)
        t_gnw = ldma(gnw_sb[:, :], gnwb)
        t_b2 = ldma(b2_sb[:, :], b2b)
        t_idb = cdma(idb_sb[:, :], identf)
        t_lmb = cdma(lmb_sb[:, :], lmaskf)
        t_oc = cdma(oc_sb[:, :], onescol)
        t_or = cdma(or_sb[:, :], onesrow)
        P.wait("scalar", t_cs)

        s1f = AR.alloc("s1f", [64, H * M], F32, side="right")
        s2f = AR.alloc("s2f", [64, H * DV], F32, side="right")
        psf = AR.alloc("psf", [1, H * M], F32, side="right")
        s1b = AR.alloc("s1b", [64, H * M], BF16, side="right")
        s2b = AR.alloc("s2b", [64, H * DV], BF16, side="right")
        psb = AR.alloc("psb", [1, H * M], BF16, side="right")
        zt = None
        for st, pr in ((s1f, 64), (s2f, 64), (psf, 1), (s1b, 64), (s2b, 64), (psb, 1)):
            zt = vop(lambda v, a=st, pr=pr: v.memset(a.h[0:pr, :], 0))
        s1b_rdy = [zt] * NG
        s2b_rdy = [zt] * NG
        psb_rdy = [zt] * NG
        s1f_rdy = [zt] * NG
        s2f_rdy = [zt] * NG
        psf_rdy = [zt] * NG

        def load_w(name, dram, rows, cols, col0=0, ncols=None, per_dma=4,
                   side="left"):
            w = cols if ncols is None else ncols
            rc = rows // 128
            buf = AR.alloc(name, [128, rc * w], BF16, side=side)
            toks = list(buf.birth)
            npc = (rc + per_dma - 1) // per_dma
            sems = [next_wsem() for _ in range(min(npc, 4))]
            ctoks = {}
            for i, k0 in enumerate(range(0, rc, per_dma)):
                nk = min(per_dma, rc - k0)
                dap = bass.AP(dram.tensor, dram.offset + (k0 * 128) * cols + col0,
                              [[cols, 128], [128 * cols, nk], [1, w]])
                sap = bass.AP(buf.h, k0 * w, [[rc * w, 128], [w, nk], [1, w]])
                t = cdma(sap, dap, waits=toks, sem=sems[i % len(sems)],
                         quiesce=True)
                toks = []
                for k in range(k0, k0 + nk):
                    ctoks[k] = t
            last = ctoks[rc - 1]

            def tok_of(k=-1, ctoks=ctoks, last=last):
                return last if k < 0 else ctoks[min(k, rc - 1)]
            return buf, tok_of

        def load_w_into(buf, dram, rows, cols, col0=0, ncols=None,
                        per_dma=4, waits=()):
            w = cols if ncols is None else ncols
            rc = rows // 128
            toks = list(waits)
            npc = (rc + per_dma - 1) // per_dma
            sems = [next_wsem() for _ in range(min(npc, 4))]
            ctoks = {}
            for i, k0 in enumerate(range(0, rc, per_dma)):
                nk = min(per_dma, rc - k0)
                dap = bass.AP(dram.tensor, dram.offset + (k0 * 128) * cols + col0,
                              [[cols, 128], [128 * cols, nk], [1, w]])
                sap = bass.AP(buf.h, k0 * w, [[buf.pitch, 128], [w, nk], [1, w]])
                t = cdma(sap, dap, waits=toks, sem=sems[i % len(sems)],
                         quiesce=True)
                toks = []
                for k in range(k0, k0 + nk):
                    ctoks[k] = t
            last = ctoks[rc - 1]

            def tok_of(k=-1, ctoks=ctoks, last=last):
                return last if k < 0 else ctoks[min(k, rc - 1)]
            return tok_of

        def gemm(lhs_fn, rhs_fn, nk, banks, ncols, waits=(), ktok=None):
            ngrp = (ncols + 511) // 512
            w0 = list(waits) + [bfree(b) for b in banks[:ngrp]]
            mt = None
            for k in range(nk):
                if ktok is not None:
                    P.wait("tensor", ktok(k))
                for g in range(ngrp):
                    cn = min(512, ncols - g * 512)
                    last = (k == nk - 1 and g == ngrp - 1)
                    mt = mm(banks[g][0:128, 0:cn], lhs_fn(k), rhs_fn(k, g),
                            start=(k == 0), stop=(k == nk - 1),
                            waits=w0 if (k == 0 and g == 0) else (),
                            inc=last)
            return mt

        strips = [AR.alloc(f"strip{i}", [128, D], F32, side="right") for i in range(2)]
        strip_free = [None, None]

        def strip_load(dram_x, row0, sb, waits=()):
            return ldma(strips[sb].ap(0, 128, 0, D), dram_x[row0:row0 + 128, :],
                        waits=list(waits) + [strip_free[sb]], sem=f"ST{sb}")

        def x_to_fm(sb, st_tok, dst, c0):
            ev = None
            for k0 in range(0, KC, 4):
                nk = min(4, KC - k0)
                bank = FB[4 + (k0 // 4) % 2]
                mt = None
                for j in range(nk):
                    mt = mm(bank[0:128, j * 128:(j + 1) * 128],
                            strips[sb].ap(0, 128, (k0 + j) * 128, 128),
                            idf_sb[0:128, 0:128], start=(j == 0), stop=(j == nk - 1),
                            waits=[st_tok, t_idf, bfree(bank)] if j == 0 else (),
                            inc=(j == nk - 1), tp=True)
                ev = vop(lambda v, b=bank, nk=nk, d=dst, o=c0 + k0 * 128:
                         v.tensor_copy(d.h[0:128, o:o + nk * 128], b[0:128, 0:nk * 128]),
                         waits=[mt] + dst.birth)
                bset(bank, ev)
            strip_free[sb] = P.cur("MM")
            return ev

        def tp_batch(srcs, w, bank, dst, dst_ap, waits=()):
            mt = None
            for j, sap in enumerate(srcs):
                mt = mm(bank[0:w, j * 128:(j + 1) * 128], sap, idb_sb[0:128, 0:128],
                        start=(j == 0), stop=(j == len(srcs) - 1),
                        waits=([t_idb, bfree(bank)] + list(waits)) if j == 0 else (),
                        inc=(j == len(srcs) - 1), tp=True)
            ev = vop(lambda v, b=bank, w=w, n=len(srcs), d=dst_ap:
                     v.tensor_copy(d, b[0:w, 0:n * 128]),
                     waits=[mt] + list(dst.birth))
            bset(bank, ev)
            return ev

        def phase_barrier():
            return [P.cur("MM"), P.cur("VE"), P.cur("SC"), P.cur("GPC")]

        wsb, twS = load_w("wsb", ws, D, MD, per_dma=2)
        wkb, twK = load_w("wkb", wk, D, QD, per_dma=2)
        wvb, twV = load_w("wvb", wv, D, VD, per_dma=2)

        xpf = [AR.alloc(f"xpf{i}", [128, KC * 128], BF16) for i in range(2)]
        xpf_free = [None, None]
        echkb = [AR.alloc("echk", [128, MD], BF16) for _ in range(2)]
        kchkb = [AR.alloc("kchk", [128, QD], BF16) for _ in range(2)]
        vchkb = [AR.alloc("vchk", [128, VD], BF16) for _ in range(2)]
        echk_freeb = [None, None]
        kchk_freeb = [None, None]
        vchk_freeb = [None, None]

        st = strip_load(x_pre, 0, 0)
        xe_next = x_to_fm(0, st, xpf[0], 0)
        GV2 = VD // 2
        a1_prev = {}
        ps_pend = None

        def ps_pass(echk_p, sb_p):
            for j in range((MD + 511) // 512):
                cn = min(512, MD - j * 512)
                jb = j % 2
                bk = FB[4 + jb]
                w = [t_oc, bfree(bk)] + (
                    [a1_prev[jb]] if jb in a1_prev else [])
                tok = mm(bk[0:1, 0:cn], oc_sb.h[0:128, 0:1],
                         echk_p.ap(0, 128, j * 512, cn),
                         start=True, stop=True, waits=w, inc=True)
                ap = vop(lambda v, j=j, cn=cn, bk=bk: v.tensor_tensor(
                    psf.ap(0, 1, j * 512, cn), psf.ap(0, 1, j * 512, cn),
                    bk[0:1, 0:cn], ALU.add), waits=[tok])
                for g in range(NG):
                    if (g * GH * M) // 512 == j:
                        psf_rdy[g] = ap
                bset(bk, ap)
            echk_freeb[sb_p] = P.cur("MM")

        for pc in range(NPC):
            sb = pc % 2
            echk, kchk, vchk = echkb[sb], kchkb[sb], vchkb[sb]
            xe = xe_next
            xf = xpf[sb]

            def xsl(k, xf=xf):
                return xf.ap(0, 128, k * 128, 128)

            mS_b = []
            for b in range((MD + 511) // 512):
                cn = min(512, MD - b * 512)
                mS_b.append(gemm(
                    xsl, lambda k, g, b=b, cn=cn: wsb.ap(
                        0, 128, k * MD + b * 512, cn),
                    KC, [FB[b]], cn,
                    waits=[xe, xpf_free[sb]] if b == 0 else (), ktok=twS))
            mK_b = []
            for b in range((QD + 511) // 512):
                cn = min(512, QD - b * 512)
                mK_b.append(gemm(
                    xsl, lambda k, g, b=b, cn=cn: wkb.ap(
                        0, 128, k * QD + b * 512, cn),
                    KC, [FB[2 + b]], cn, ktok=twK))
            ee = None
            for g in range((MD + 511) // 512):
                cn = min(512, MD - g * 512)
                cl = vop(lambda v, g=g, cn=cn: v.tensor_scalar(
                    FB[g][0:128, 0:cn], FB[g][0:128, 0:cn],
                    CLM, CLX, ALU.max, ALU.min), waits=[mS_b[g]])
                ee = sop(lambda s, g=g, cn=cn, echk=echk: s.activation(
                    echk.ap(0, 128, g * 512, cn), FB[g][0:128, 0:cn], AF.Exp,
                    bias=pb_sb.h[0:128, 0:1], scale=1.0),
                    waits=[cl, t_pb, echk_freeb[sb]] + echk.birth)
                bset(FB[g], ee)
            ke = None
            for g in range((QD + 511) // 512):
                cn = min(512, QD - g * 512)
                ke = sop(lambda s, g=g, cn=cn, kchk=kchk: s.copy(
                    kchk.ap(0, 128, g * 512, cn), FB[2 + g][0:128, 0:cn]),
                    waits=[mK_b[g], kchk_freeb[sb]] + kchk.birth)
                bset(FB[2 + g], ke)
            if ps_pend is not None:
                ps_pass(*ps_pend)
                ps_pend = None
            if pc + 1 < NPC:
                nsb = (pc + 1) % 2
                stn = strip_load(x_pre, (pc + 1) * 128, nsb)
                xe_next = x_to_fm(nsb, stn, xpf[nsb], 0)
            s2w = GH * DV
            s2tot = NG * s2w
            ve_tok = [None, None]
            for vh in range(2):
                bks = [FB[0], FB[1]] if vh == 0 else [FB[2], FB[3]]
                mV = gemm(xsl,
                          lambda k, g, vh=vh: wvb.ap(
                              0, 128, k * VD + vh * GV2 + g * 512,
                              min(512, GV2 - g * 512)),
                          KC, bks, GV2,
                          waits=[ee] if vh == 0 else [ke], ktok=twV)
                if vh == 1:
                    xpf_free[sb] = mV
                ve = None
                for g in range((GV2 + 511) // 512):
                    cn = min(512, GV2 - g * 512)
                    ve = vop(lambda v, g=g, cn=cn, vh=vh, vchk=vchk, bks=bks:
                             v.tensor_copy(
                        vchk.ap(0, 128, vh * GV2 + g * 512, cn),
                        bks[g][0:128, 0:cn]),
                        waits=[mV, vchk_freeb[sb]] + vchk.birth)
                for g in range((GV2 + 511) // 512):
                    bset(bks[g], ve)
                ve_tok[vh] = ve
            s1w = GH * M
            s1tot = NG * s1w
            s1last = {}
            for g in range(NG):
                flat = g * s1w
                b, co = flat // 512, flat % 512
                bk = FB[4 + b]
                lastb = (flat + s1w == min((b + 1) * 512, s1tot))
                for i in range(GH):
                    bh = g * GH + i
                    fi = (co == 0 and i == 0)
                    la = (lastb and i == GH - 1)
                    tok = mm(bk[0:64, co + i * M: co + (i + 1) * M],
                             kchk.ap(0, 128, bh * DK, DK),
                             echk.ap(0, 128, bh * M, M),
                             start=fi, stop=la,
                             waits=[ee, ke, bfree(bk)] if fi else (),
                             inc=la)
                    if la:
                        s1last[b] = tok
            a1toks = {}
            for b in sorted(s1last):
                cn = min(512, s1tot - b * 512)
                a1toks[b] = vop(lambda v, b=b, cn=cn: v.tensor_tensor(
                    s1f.ap(0, 64, b * 512, cn), s1f.ap(0, 64, b * 512, cn),
                    FB[4 + b][0:64, 0:cn], ALU.add), waits=[s1last[b]])
            for g in range(NG):
                s1f_rdy[g] = a1toks[(g * s1w) // 512]
            a1_prev = a1toks
            for vh in range(2):
                bks = [FB[0], FB[1]] if vh == 0 else [FB[2], FB[3]]
                ve = ve_tok[vh]
                s2last = {}
                h_lo = vh * (H // 2)
                for idx in range(H // 2):
                    bh = h_lo + idx
                    flat = idx * DV
                    b, co = flat // 512, flat % 512
                    bk = bks[b]
                    fi = (co == 0)
                    la = (flat + DV == min((b + 1) * 512, s2tot // 2))
                    tok = mm(bk[0:64, co:co + DV],
                             echk.ap(0, 128, bh * M, M),
                             vchk.ap(0, 128, bh * DV, DV),
                             start=fi, stop=la,
                             waits=[ee, ve, bfree(bk)] if fi else (),
                             inc=la)
                    if la:
                        s2last[b] = tok
                for b in sorted(s2last):
                    cn = min(512, s2tot // 2 - b * 512)
                    a2 = vop(lambda v, b=b, cn=cn, vh=vh, bks=bks:
                             v.tensor_tensor(
                        s2f.ap(0, 64, vh * (s2tot // 2) + b * 512, cn),
                        s2f.ap(0, 64, vh * (s2tot // 2) + b * 512, cn),
                        bks[b][0:64, 0:cn], ALU.add), waits=[s2last[b]])
                    bset(bks[b], a2)
                    for g in range(NG):
                        for i in range(GH):
                            fl = (g * GH + i) * DV - vh * (s2tot // 2)
                            if 0 <= fl < s2tot // 2 and fl // 512 == b:
                                s2f_rdy[g] = a2
            ps_pend = (echk, sb)
            kchk_freeb[sb] = P.cur("MM")
            vchk_freeb[sb] = P.cur("MM")
        ps_pass(*ps_pend)

        for g in range(NG):
            s1b_rdy[g] = vop(lambda v, g=g: v.tensor_copy(
                s1b.ap(0, 64, g * GH * M, GH * M),
                s1f.ap(0, 64, g * GH * M, GH * M)), waits=[s1f_rdy[g]])
            s2b_rdy[g] = vop(lambda v, g=g: v.tensor_copy(
                s2b.ap(0, 64, g * GH * DV, GH * DV),
                s2f.ap(0, 64, g * GH * DV, GH * DV)), waits=[s2f_rdy[g]])
            psb_rdy[g] = vop(lambda v, g=g: v.tensor_copy(
                psb.ap(0, 1, g * GH * M, GH * M),
                psf.ap(0, 1, g * GH * M, GH * M)), waits=[psf_rdy[g]])

        pb_end = phase_barrier()
        for b in (vchkb[0], vchkb[1], kchkb[0], kchkb[1], echkb[0], echkb[1],
                  xpf[1], xpf[0]):
            AR.freebuf(b, toks=pb_end)
        _spare = AR.alloc("spare", [128, 16 * 1024], BF16)
        AR.freebuf(_spare, toks=[])

        utb = v2tb = w2b = b1_sb = None
        tUT = tV2 = tW2 = tB1 = None

        pre_xmf = [None]
        ws_pre = [None]
        h1_pre = [None, None]
        for half in range(2):
            tok0 = half * TH

            ovhh = [None, None]

            def ov_ap(c, col, width):
                b = col // GV
                assert col % GV + width <= GV
                return ovhh[b].ap(0, 128, c * GV + col % GV, width)
            if half == 1 and pre_xmf[0] is not None:
                xmf = pre_xmf[0]
            else:
                xmf = AR.alloc("xmf", [128, NHC * KC * 128], BF16)
            eh = AR.alloc("eh", [128, NHC * MD], BF16)

            def xslc(c):
                def f(k, c=c):
                    return xmf.ap(0, 128, (c * KC + k) * 128, 128)
                return f

            if half == 0:
                wsb2, tws2 = wsb, twS
            elif ws_pre[0] is not None:
                wsb2, tws2 = ws_pre[0]
            else:
                wsb2, tws2 = load_w("wsb2", ws, D, MD)
            xe_tok = [None] * NHC
            ee_tok = [None] * NHC
            ke_tok = [None] * NHC
            ve_tok = [None] * NHC
            qe_tok = [None] * NHC
            for c in range(NHC):
                bk = [FB[0], FB[1]] if c % 2 == 0 else [FB[2], FB[3]]
                sb = c % 2
                st = strip_load(x_main, tok0 + c * 128, sb)
                xe = x_to_fm(sb, st, xmf, c * KC * 128)
                xe_tok[c] = xe
                mS = gemm(xslc(c), lambda k, g: wsb2.ap(0, 128, k * MD + g * 512,
                                                        min(512, MD - g * 512)),
                          KC, bk, MD, waits=[xe], ktok=tws2)
                cl = None
                for g in range((MD + 511) // 512):
                    cn = min(512, MD - g * 512)
                    cl = vop(lambda v, g=g, cn=cn, bk=bk: v.tensor_scalar(
                        bk[g][0:128, 0:cn], bk[g][0:128, 0:cn],
                        CLM, CLX, ALU.max, ALU.min), waits=[mS])
                ee = None
                for g in range((MD + 511) // 512):
                    cn = min(512, MD - g * 512)
                    ee = sop(lambda s, c=c, g=g, cn=cn, bk=bk: s.activation(
                        eh.ap(0, 128, c * MD + g * 512, cn),
                        bk[g][0:128, 0:cn], AF.Exp,
                        bias=0.0, scale=1.0), waits=[cl] + eh.birth)
                for g in range((MD + 511) // 512):
                    bset(bk[g], ee)
                ee_tok[c] = ee
            bS = phase_barrier()
            AR.freebuf(wsb2, toks=bS)
            wqb2, twq2 = load_w("wqb2", wq, D, QD)
            if half == 0:
                wkb2, twk2 = wkb, twK
                wv_pre = [None, None]
            else:
                wkb2, twk2 = h1_pre[0]
                wv_pre = [h1_pre[1], None]

            ktm = AR.alloc("ktm", [128, NHC * QD], BF16)
            for c in range(NHC):
                bk = [FB[0], FB[1]] if c % 2 == 0 else [FB[2], FB[3]]
                mK = gemm(xslc(c), lambda k, g: wkb2.ap(0, 128, k * QD + g * 512,
                                                        min(512, QD - g * 512)),
                          KC, bk, QD, waits=[xe_tok[c]], ktok=twk2)
                ke = None
                for g in range((QD + 511) // 512):
                    cn = min(512, QD - g * 512)
                    ke = sop(lambda s, c=c, g=g, cn=cn, bk=bk: s.copy(
                        ktm.ap(0, 128, c * QD + g * 512, cn), bk[g][0:128, 0:cn]),
                        waits=[mK] + ktm.birth)
                for g in range((QD + 511) // 512):
                    bset(bk[g], ke)
                ke_tok[c] = ke
            bK = phase_barrier()
            AR.freebuf(wkb2, toks=bK)
            del bK
            if half == 1:
                wv_pre[1] = load_w("wvb2", wv, D, VD, col0=GV, ncols=GV)

            wg_pre = [None, None]
            vtm = AR.alloc("vtm", [128, NHC * VD], BF16)
            for vh in range(2):
                if half == 0:
                    wvb2, twv2 = wvb, twV
                    wv_ap = (lambda k, g, vh=vh: wvb.ap(
                        0, 128, k * VD + vh * GV + g * 512,
                        min(512, GV - g * 512)))
                else:
                    wvb2, twv2 = wv_pre[vh]
                    wv_ap = (lambda k, g, w=wvb2: w.ap(
                        0, 128, k * GV + g * 512, min(512, GV - g * 512)))
                vco = 0
                for c in range(NHC):
                    bk = [FB[0], FB[1]] if c % 2 == 0 else [FB[2], FB[3]]
                    mV = gemm(xslc(c), wv_ap,
                              KC, bk, GV, waits=[xe_tok[c]], ktok=twv2)
                    ve = None
                    for g in range((GV + 511) // 512):
                        cn = min(512, GV - g * 512)
                        ve = sop(lambda s, c=c, g=g, cn=cn, vh=vh, bk=bk:
                                 s.copy(
                            vtm.ap(0, 128, c * VD + vh * GV + g * 512, cn),
                            bk[g][0:128, 0:cn]), waits=[mV] + vtm.birth)
                    for g in range((GV + 511) // 512):
                        bset(bk[g], ve)
                    ve_tok[c] = ve
                bV = phase_barrier()
                if half == 1 or vh == 1:
                    AR.freebuf(wvb2, toks=bV)

            qtm = AR.alloc("qtm", [128, NHC * QD], BF16)
            for c in range(NHC):
                bk = [FB[0], FB[1]] if c % 2 == 0 else [FB[2], FB[3]]
                mQ = gemm(xslc(c), lambda k, g: wqb2.ap(0, 128, k * QD + g * 512,
                                                        min(512, QD - g * 512)),
                          KC, bk, QD, waits=[xe_tok[c]], ktok=twq2)
                qe = None
                for g in range((QD + 511) // 512):
                    cn = min(512, QD - g * 512)
                    qe = sop(lambda s, c=c, g=g, cn=cn, bk=bk: s.activation(
                        qtm.ap(0, 128, c * QD + g * 512, cn), bk[g][0:128, 0:cn],
                        AF.Copy, bias=0.0, scale=QSC), waits=[mQ] + qtm.birth)
                for g in range((QD + 511) // 512):
                    bset(bk[g], qe)
                qe_tok[c] = qe
            bQ = phase_barrier()
            AR.freebuf(wqb2, toks=bQ)
            wg_pre[0] = load_w("wgb2", wg, D, VD, col0=0, ncols=GV)

            ovhh[0] = AR.alloc("ovh", [128, NHC * GV], F32)
            ovhh[1] = AR.alloc("ovh", [128, NHC * GV], F32)
            kqfmb = [AR.alloc("kqfm", [64, 2 * GH * 128], BF16) for _ in range(2)]
            efmb = [AR.alloc("efm", [64, GH * 128], BF16) for _ in range(3)]
            cfmb = [AR.alloc("cfm", [64, GH * 128], BF16) for _ in range(2)]
            abfb = [AR.alloc("abf", [128, GH * 128], BF16) for _ in range(2)]
            wbfb = [AR.alloc("wbf", [128, GH * 128], BF16) for _ in range(2)]
            rbuf = AR.alloc("rbuf", [128, GH * M], F32)
            okb = AR.alloc("okb", [128, GH * M], F32)
            ebuf = AR.alloc("ebuf", [128, GH * M], F32)
            mx4 = AR.alloc("mx4", [128, GH], F32)
            sm4 = AR.alloc("sm4", [128, GH], F32)
            cbfb = [AR.alloc("cbf", [128, GH * M], BF16) for _ in range(2)]
            arwb = [AR.alloc("arw", [128, GH * 128], BF16) for _ in range(2)]
            wrwb = [AR.alloc("wrw", [128, GH * 128], BF16) for _ in range(2)]
            NIT = NHC * NG
            tok_te = [None] * NIT
            tok_eA = [None] * NIT
            tok_mtN = [None] * NIT
            tok_cC = [None] * NIT
            tok_tc = [None] * NIT
            ex_prev = [None]

            def scp(out_ap, in_ap, waits=()):
                return sop(lambda s, o=out_ap, i=in_ap: s.copy(o, i), waits=waits)

            def p_front(it):
                c, g = divmod(it, NG)
                h0 = g * GH
                kqf = kqfmb[it % 2]
                efm = efmb[it % 3]
                srcs = [ktm.ap(0, 128, c * QD + (h0 + j) * DK, DK)
                        for j in range(GH)] + \
                       [qtm.ap(0, 128, c * QD + (h0 + j) * DK, DK)
                        for j in range(GH)]
                mt = None
                for j, sap in enumerate(srcs):
                    mt = mm(TBK[0][0:64, j * 128:(j + 1) * 128], sap,
                            idb_sb.h[0:128, 0:128], start=(j == 0),
                            stop=(j == len(srcs) - 1),
                            waits=([t_idb, bfree(TBK[0]), ke_tok[c],
                                    qe_tok[c]]) if j == 0 else (),
                            inc=(j == len(srcs) - 1), tp=True)
                ev = scp(kqf.ap(0, 64, 0, 2 * GH * 128),
                         TBK[0][0:64, 0:2 * GH * 128], waits=[mt] + kqf.birth)
                bset(TBK[0], ev)
                mt = None
                for j in range(GH):
                    mt = mm(TBK[1][0:64, j * 128:(j + 1) * 128],
                            eh.ap(0, 128, c * MD + (h0 + j) * M, M),
                            idb_sb.h[0:128, 0:128], start=(j == 0),
                            stop=(j == GH - 1),
                            waits=[t_idb, bfree(TBK[1]), ee_tok[c]]
                            if j == 0 else (),
                            inc=(j == GH - 1), tp=True)
                ev2 = scp(efm.ap(0, 64, 0, GH * 128), TBK[1][0:64, 0:GH * 128],
                          waits=[mt] + efm.birth)
                bset(TBK[1], ev2)
                tok_te[it] = ev2
                mtA = None
                for j in range(GH):
                    mtA = mm(FB[0][0:128, j * 128:(j + 1) * 128],
                             kqf.ap(0, 64, j * 128, 128),
                             kqf.ap(0, 64, GH * 128 + j * 128, 128),
                             start=(j == 0), stop=(j == GH - 1),
                             waits=[ev, bfree(FB[0])] if j == 0 else (),
                             inc=(j == GH - 1))
                abf = abfb[it % 2]
                arw = arwb[it % 2]
                cA = sop(lambda s, a=arw: s.copy(a.ap(0, 128, 0, GH * 128),
                                                 FB[0][0:128, 0:GH * 128]),
                         waits=[mtA] + arw.birth)
                bset(FB[0], cA)
                P.waits("gpsimd", [cA, t_lmb] + abf.birth)
                eA = P.inc("gpsimd", "GPC", _eager(
                    lambda g, a=abf, r=arw: g.tensor_tensor(
                        a.ap(0, 128, 0, GH * 128), r.ap(0, 128, 0, GH * 128),
                        lmb_sb.ap3(0, 128, 0, [[0, GH], [1, 128]]), ALU.mult)))
                tok_eA[it] = eA

            def p_np(it):
                c, g = divmod(it, NG)
                h0 = g * GH
                kqf = kqfmb[it % 2]
                abf = abfb[it % 2]
                NPB = FB[2] if it % 2 == 0 else FB[3]
                wN = [tok_eA[it], s1b_rdy[g], psb_rdy[g], t_lmb, t_or,
                      bfree(NPB), ee_tok[c]]
                first = True
                for j in range(GH):
                    bh = h0 + j
                    mm(NPB[0:128, j * M:(j + 1) * M],
                       kqf.ap(0, 64, GH * 128 + j * 128, 128),
                       s1b.ap(0, 64, bh * M, M), start=first, stop=False,
                       waits=wN if first else ())
                    first = False
                mm(NPB[0:128, GH * M:2 * GH * M], lmb_sb.h[0:128, 0:128],
                   eh.ap(0, 128, c * MD + h0 * M, GH * M), start=False,
                   stop=False)
                mm(NPB[0:128, GH * M:2 * GH * M], or_sb.h[0:1, 0:128],
                   psb.ap(0, 1, h0 * M, GH * M), start=False, stop=False)
                mtN = None
                for j in range(GH):
                    bh = h0 + j
                    last = (j == GH - 1)
                    mtN = mm(NPB[0:128, j * M:(j + 1) * M],
                             abf.ap(0, 128, j * 128, 128),
                             eh.ap(0, 128, c * MD + bh * M, M),
                             start=False, stop=last, inc=last)
                tok_mtN[it] = mtN

            def p_pv(it):
                NPB = FB[2] if it % 2 == 0 else FB[3]
                cbf = cbfb[it % 2]
                lnp = sop(lambda s: s.activation(
                    rbuf.ap(0, 128, 0, GH * M), NPB[0:128, GH * M:2 * GH * M],
                    AF.Ln, bias=0.0, scale=1.0),
                    waits=[tok_mtN[it]] + rbuf.birth)
                vop(lambda v: v.memset(rbuf.h[0:1, 0:1], 0)) if False else None
                sop(lambda s: s.activation(
                    rbuf.ap(0, 128, 0, GH * M), rbuf.ap(0, 128, 0, GH * M),
                    AF.Exp, bias=0.0, scale=-1.0), waits=[lnp])
                ok = vop(lambda v: v.tensor_tensor(
                    okb.ap(0, 128, 0, GH * M), NPB[0:128, 0:GH * M],
                    rbuf.ap(0, 128, 0, GH * M), ALU.mult),
                    waits=[ex_prev[0], P.cur("SC")] + okb.birth)
                bset(NPB, ok)
                ex = sop(lambda s: s.activation(
                    ebuf.ap(0, 128, 0, GH * M), okb.ap(0, 128, 0, GH * M),
                    AF.Exp, bias=0.0, scale=1.0),
                    waits=[P.cur("VE")] + ebuf.birth)
                ex_prev[0] = ex
                vop(lambda v: v.tensor_reduce(
                    sm4.ap(0, 128, 0, GH),
                    ebuf.ap3(0, 128, 0, [[M, GH], [1, M]]), AX.X, ALU.add),
                    waits=[ex] + sm4.birth)
                vop(lambda v: v.reciprocal(sm4.ap(0, 128, 0, GH),
                                           sm4.ap(0, 128, 0, GH)))
                vop(lambda v: v.tensor_tensor(
                    ebuf.ap(0, 128, 0, GH * M), ebuf.ap(0, 128, 0, GH * M),
                    rbuf.ap(0, 128, 0, GH * M), ALU.mult))
                tok_cC[it] = vop(lambda v, cb=cbf: v.tensor_tensor(
                    cb.ap(0, 128, 0, GH * M), ebuf.ap(0, 128, 0, GH * M),
                    sm4.ap3(0, 128, 0, [[1, GH], [0, M]]), ALU.mult),
                    waits=([tok_tc[it - 2]] if it >= 2 else []) + cbf.birth)

            def p_back(it):
                c, g = divmod(it, NG)
                h0 = g * GH
                efm = efmb[it % 3]
                cbf = cbfb[it % 2]
                cfm = cfmb[it % 2]
                wbf = wbfb[it % 2]
                mt = None
                for j in range(GH):
                    mt = mm(TBK[1][0:64, j * 128:(j + 1) * 128],
                            cbf.ap(0, 128, j * M, M), idb_sb.h[0:128, 0:128],
                            start=(j == 0), stop=(j == GH - 1),
                            waits=([tok_cC[it], t_idb, bfree(TBK[1])]
                                   if j == 0 else ()),
                            inc=(j == GH - 1), tp=True)
                tc = scp(cfm.ap(0, 64, 0, GH * 128), TBK[1][0:64, 0:GH * 128],
                         waits=[mt] + cfm.birth)
                bset(TBK[1], tc)
                tok_tc[it] = tc
                mtW = None
                for j in range(GH):
                    mtW = mm(FB[4][0:128, j * 128:(j + 1) * 128],
                             efm.ap(0, 64, j * 128, 128),
                             cfm.ap(0, 64, j * 128, 128),
                             start=(j == 0), stop=(j == GH - 1),
                             waits=([tc, tok_te[it], bfree(FB[4])]
                                    if j == 0 else ()),
                             inc=(j == GH - 1))
                wrw = wrwb[it % 2]
                cW = sop(lambda s, w=wrw: s.copy(w.ap(0, 128, 0, GH * 128),
                                                 FB[4][0:128, 0:GH * 128]),
                         waits=[mtW] + wrw.birth)
                bset(FB[4], cW)
                P.waits("gpsimd", [cW, t_lmb] + wbf.birth)
                eW = P.inc("gpsimd", "GPC", _eager(
                    lambda g, w=wbf, r=wrw: g.tensor_tensor(
                        w.ap(0, 128, 0, GH * 128), r.ap(0, 128, 0, GH * 128),
                        lmb_sb.ap3(0, 128, 0, [[0, GH], [1, 128]]), ALU.mult)))
                first = True
                for j in range(GH):
                    bh = h0 + j
                    mm(FB[5][0:128, j * 128:(j + 1) * 128],
                       cfm.ap(0, 64, j * 128, 128), s2b.ap(0, 64, bh * DV, DV),
                       start=first, stop=False,
                       waits=[tc, s2b_rdy[g], bfree(FB[5]), ve_tok[c]]
                       if first else ())
                    first = False
                mtO = None
                for j in range(GH):
                    bh = h0 + j
                    last = (j == GH - 1)
                    mtO = mm(FB[5][0:128, j * 128:(j + 1) * 128],
                             wbf.ap(0, 128, j * 128, 128),
                             vtm.ap(0, 128, c * VD + bh * DV, DV),
                             start=False, stop=last,
                             waits=[eW] if j == 0 else (), inc=last)
                eO = None
                col0 = g * GH * DV
                off = 0
                while off < GH * DV:
                    w = min(GH * DV - off, GV - (col0 + off) % GV)
                    eO = scp(ov_ap(c, col0 + off, w),
                             FB[5][0:128, off:off + w],
                             waits=[mtO] + ovhh[0].birth + ovhh[1].birth)
                    off += w
                bset(FB[5], eO)
                mt2 = None
                for j in range(GH):
                    bh = h0 + j
                    last = (j == GH - 1)
                    mt2 = mm(FB[4][0:64, j * DV:(j + 1) * DV],
                             eh.ap(0, 128, c * MD + bh * M, M),
                             vtm.ap(0, 128, c * VD + bh * DV, DV),
                             start=(j == 0), stop=last,
                             waits=[eW] if j == 0 else (), inc=last)
                wS1 = [bfree(FB[1]), t_oc, ke_tok[c]]
                for j in range(GH - 1):
                    bh = h0 + j
                    mm(FB[1][0:64, j * M:(j + 1) * M],
                       ktm.ap(0, 128, c * QD + bh * DK, DK),
                       eh.ap(0, 128, c * MD + bh * M, M),
                       start=(j == 0), stop=False, waits=wS1 if j == 0 else ())
                mm(FB[1][0:1, GH * M:2 * GH * M], oc_sb.h[0:128, 0:1],
                   eh.ap(0, 128, c * MD + h0 * M, GH * M),
                   start=False, stop=False, waits=wS1 if GH == 1 else ())
                bh = h0 + GH - 1
                mt1 = mm(FB[1][0:64, (GH - 1) * M:GH * M],
                         ktm.ap(0, 128, c * QD + bh * DK, DK),
                         eh.ap(0, 128, c * MD + bh * M, M),
                         start=False, stop=True, inc=True)
                a1 = vop(lambda v, g=g: v.tensor_tensor(
                    s1f.ap(0, 64, g * GH * M, GH * M),
                    s1f.ap(0, 64, g * GH * M, GH * M),
                    FB[1][0:64, 0:GH * M], ALU.add), waits=[mt1, s1f_rdy[g]])
                a2 = vop(lambda v, g=g: v.tensor_tensor(
                    psf.ap(0, 1, g * GH * M, GH * M),
                    psf.ap(0, 1, g * GH * M, GH * M),
                    FB[1][0:1, GH * M:2 * GH * M], ALU.add),
                    waits=[psf_rdy[g]])
                s1f_rdy[g] = a1
                psf_rdy[g] = a2
                bset(FB[1], a2)
                a3 = vop(lambda v, g=g: v.tensor_tensor(
                    s2f.ap(0, 64, g * GH * DV, GH * DV),
                    s2f.ap(0, 64, g * GH * DV, GH * DV),
                    FB[4][0:64, 0:GH * DV], ALU.add), waits=[mt2, s2f_rdy[g]])
                s2f_rdy[g] = a3
                bset(FB[4], a3)
                if g == NG - 1:
                    w1s = [s1f_rdy[gg] for gg in range(NG)] + [tok_mtN[it]]
                    t1 = scp(s1b.ap(0, 64, 0, H * M), s1f.ap(0, 64, 0, H * M),
                             waits=w1s)
                    w2s = [s2f_rdy[gg] for gg in range(NG)] + [mtO]
                    t2 = scp(s2b.ap(0, 64, 0, H * DV), s2f.ap(0, 64, 0, H * DV),
                             waits=w2s)
                    w3s = [psf_rdy[gg] for gg in range(NG)] + [tok_mtN[it]]
                    t3 = scp(psb.ap(0, 1, 0, H * M), psf.ap(0, 1, 0, H * M),
                             waits=w3s)
                    for gg in range(NG):
                        s1b_rdy[gg] = t1
                        s2b_rdy[gg] = t2
                        psb_rdy[gg] = t3

            for it in range(NIT + 1):
                if it < NIT:
                    p_front(it)
                if 0 <= it - 1 < NIT:
                    p_back(it - 1)
                if it < NIT:
                    p_np(it)
                    p_pv(it)
            bPAT = phase_barrier()
            for b in (cbfb + arwb + wrwb + [sm4, mx4, ebuf, okb, rbuf]
                      + wbfb + abfb + cfmb + efmb + kqfmb
                      + [qtm, ktm, eh, vtm]):
                AR.freebuf(b, toks=bPAT)

            ofm = AR.alloc("ofm", [128, VDC * TH], BF16)
            gsc2 = [AR.alloc("gsc", [128, GV], F32) for _ in range(2)]
            gsc2_free = [None, None]
            sgs = AR.alloc("sgs", [128, GV], F32)
            obs = AR.alloc("obs", [128, GV], F32)
            otb = AR.alloc("otb", [128, GV], BF16)
            otb_free = None
            sgs_free = [None]
            msq = AR.alloc("msq", [128, NHC * HH], F32)
            ofm_tok = [None] * NHC
            wg_pre[1] = load_w("wgb2", wg, D, VD, col0=GV, ncols=GV)
            wo_pre = [None, None]
            for vh in range(2):
                if vh == 1:
                    wo_pre[0] = load_w("wob2", wo, VD, D, col0=0, ncols=D2)
                wgb2, twg2 = wg_pre[vh]
                sq = None
                obs_free = None
                for c in range(NHC):
                    sq = sop(lambda s, c=c, vh=vh: s.activation(
                        obs.ap(0, 128, 0, GV),
                        ov_ap(c, vh * GV, GV), AF.Square,
                        bias=0.0, scale=1.0),
                        waits=bPAT + [obs_free] + obs.birth)
                    obs_free = vop(lambda v, c=c: v.tensor_reduce(
                        msq.ap(0, 128, c * HH, HH),
                        obs.ap3(0, 128, 0, [[DV, HH], [1, DV]]), AX.X, ALU.add),
                        waits=[sq] + msq.birth)
                si = sop(lambda s: s.activation(msq.ap(0, 128, 0, NHC * HH),
                                                msq.ap(0, 128, 0, NHC * HH),
                                                AF.Sqrt, bias=EPS, scale=1.0 / DV),
                         waits=[P.cur("VE")])
                iv = vop(lambda v: v.reciprocal(msq.ap(0, 128, 0, NHC * HH),
                                                msq.ap(0, 128, 0, NHC * HH)),
                         waits=[si])
                mG_tok = [None] * NHC
                gv_tok = [None] * NHC

                mG_of = [None] * NHC

                def g_proj(c):
                    bk = [FB[0], FB[1]] if c % 2 == 0 else [FB[2], FB[3]]
                    toks = []
                    for b in range((GV + 511) // 512):
                        cn = min(512, GV - b * 512)
                        toks.append(gemm(
                            xslc(c),
                            lambda k, g, b=b, cn=cn: wgb2.ap(
                                0, 128, k * GV + b * 512, cn),
                            KC, [bk[b]], cn,
                            waits=[xe_tok[c]] if b == 0 else (), ktok=twg2))
                    mG_of[c] = toks

                def g_evac(c):
                    bk = [FB[0], FB[1]] if c % 2 == 0 else [FB[2], FB[3]]
                    gvs = []
                    for g in range((GV + 511) // 512):
                        cn = min(512, GV - g * 512)
                        gvs.append(vop(
                            lambda v, g=g, cn=cn, bk=bk, c=c: v.tensor_copy(
                                gsc2[c % 2].ap(0, 128, g * 512, cn),
                                bk[g][0:128, 0:cn]),
                            waits=[mG_of[c][g], gsc2_free[c % 2]]
                            + gsc2[c % 2].birth))
                    for g in range((GV + 511) // 512):
                        bset(bk[g], gvs[-1])
                    gv_tok[c] = gvs

                def g_gate(c, vh):
                    gsc = gsc2[c % 2]
                    sg = None
                    for g in range((GV + 511) // 512):
                        cn = min(512, GV - g * 512)
                        sg = sop(lambda s, g=g, cn=cn: s.activation(
                            sgs.ap(0, 128, g * 512, cn),
                            gsc.ap(0, 128, g * 512, cn),
                            AF.Silu, bias=0.0, scale=1.0),
                            waits=[gv_tok[c][g], sgs_free[0]] + sgs.birth)
                    gsc2_free[c % 2] = sg
                    gop(lambda g, c=c, vh=vh: g.tensor_tensor(
                        obs.ap(0, 128, 0, GV),
                        ov_ap(c, vh * GV, GV),
                        msq.ap3(0, 128, c * HH, [[1, HH], [0, DV]]),
                        ALU.mult), waits=[iv, sgs_free[0]] + obs.birth)
                    o2 = gop(lambda g: g.tensor_tensor(
                        obs.ap(0, 128, 0, GV), obs.ap(0, 128, 0, GV),
                        gnw_sb.ap3(0, 128, 0, [[0, HH], [1, DV]]),
                        ALU.mult), waits=[t_gnw])
                    o3 = vop(lambda v: v.tensor_tensor(
                        otb.ap(0, 128, 0, GV), obs.ap(0, 128, 0, GV),
                        sgs.ap(0, 128, 0, GV), ALU.mult),
                        waits=[sg, o2, otb_free] + otb.birth)
                    sgs_free[0] = o3
                    for j0 in range(0, HH, 4):
                        nj = min(4, HH - j0)
                        bnk = TBK[j0 // 4 % 2]
                        mt = None
                        for j in range(nj):
                            mt = mm(bnk[0:128, j * 128:(j + 1) * 128],
                                    otb.ap(0, 128, (j0 + j) * DV, DV),
                                    idb_sb.h[0:128, 0:128],
                                    start=(j == 0), stop=(j == nj - 1),
                                    waits=[o3, t_idb, bfree(bnk)] if j == 0 else (),
                                    inc=(j == nj - 1), tp=True)
                        evt = vop(lambda v, b=bnk, nj=nj, c=c, j0=j0, vh=vh:
                                  v.tensor_copy(
                                      ofm.ap3(0, 128,
                                              (vh * HH + j0) * TH + c * 128,
                                              [[TH, nj], [1, 128]]),
                                      b[0:128, 0:nj * 128]),
                                  waits=[mt] + ofm.birth)
                        bset(bnk, evt)
                        ofm_tok[c] = evt
                    return P.cur("MM")

                for c in range(NHC + 1):
                    if c < NHC:
                        g_proj(c)
                    if c - 1 >= 0:
                        otb_free = g_gate(c - 1, vh)
                    if c < NHC:
                        g_evac(c)
                bPAT = phase_barrier()
                AR.freebuf(wgb2, toks=bPAT)
                AR.freebuf(ovhh[vh], toks=bPAT)

            for b in (msq, otb, obs, sgs, gsc2[0], gsc2[1]):
                AR.freebuf(b, toks=bPAT)
            AR.freebuf(xmf, toks=bPAT)

            x1bufs = AR.alloc("x1bufs", [128, NHC * D], F32)
            xa_tok = [None] * NHC
            stt = AR.alloc("stt", [128, 4 * NHC], F32)
            scrf = AR.alloc("scrf", [128, D], F32)
            scrb = AR.alloc("scrb", [128, D], BF16)
            scrf_free = scrb_free = None
            sh_free = [None, None]
            wo_pre[1] = load_w("wob2", wo, VD, D, col0=D2, ncols=D2)
            for dh in range(2):
                wob2, two2 = wo_pre[dh]
                for c in range(NHC):
                    bk = [FB[0], FB[1]] if c % 2 == 0 else [FB[2], FB[3]]
                    mW_b = []
                    for b in range((D2 + 511) // 512):
                        cn = min(512, D2 - b * 512)
                        mW_b.append(gemm(
                            lambda j, c=c: ofm.ap(0, 128, j * TH + c * 128, 128),
                            lambda j, g, b=b, cn=cn: wob2.ap(
                                0, 128, j * D2 + b * 512, cn),
                            VDC, [bk[b]], cn,
                            waits=[ofm_tok[c]] if b == 0 else (), ktok=two2))
                    sb = c % 2
                    rst = strip_load(x_main, tok0 + c * 128, sb)
                    xa = None
                    for g in range((D2 + 511) // 512):
                        cn = min(512, D2 - g * 512)
                        xa = vop(lambda v, c=c, g=g, cn=cn, dh=dh, sb=sb, bk=bk:
                                 v.tensor_tensor(
                                     x1bufs.ap(0, 128, c * D + dh * D2 + g * 512, cn),
                                     bk[g][0:128, 0:cn],
                                     strips[sb].ap(0, 128, dh * D2 + g * 512, cn),
                                     ALU.add),
                                 waits=[mW_b[g], rst] + x1bufs.birth)
                    for g in range((D2 + 511) // 512):
                        bset(bk[g], xa)
                    strip_free[sb] = xa
                    ph = c % 2
                    if dh == 0:
                        vop(lambda v, c=c: v.tensor_reduce(
                            stt.ap(0, 128, c, 1), x1bufs.ap(0, 128, c * D, D2),
                            AX.X, ALU.add), waits=[xa] + stt.birth)
                        sq = sop(lambda s, c=c, ph=ph: s.activation(
                            scrf.ap(0, 128, ph * D2, D2),
                            x1bufs.ap(0, 128, c * D, D2),
                            AF.Square, bias=0.0, scale=1.0),
                            waits=[xa, sh_free[ph]] + scrf.birth)
                        sh_free[ph] = vop(lambda v, c=c, ph=ph: v.tensor_reduce(
                            stt.ap(0, 128, NHC + c, 1),
                            scrf.ap(0, 128, ph * D2, D2),
                            AX.X, ALU.add), waits=[sq])
                    else:
                        xa_tok[c] = xa
                        vop(lambda v, c=c: v.tensor_reduce(
                            stt.ap(0, 128, 2 * NHC + c, 1),
                            x1bufs.ap(0, 128, c * D + D2, D2),
                            AX.X, ALU.add), waits=[xa])
                        vop(lambda v, c=c: v.tensor_tensor(
                            stt.ap(0, 128, c, 1), stt.ap(0, 128, c, 1),
                            stt.ap(0, 128, 2 * NHC + c, 1), ALU.add))
                        sq = sop(lambda s, c=c, ph=ph: s.activation(
                            scrf.ap(0, 128, ph * D2, D2),
                            x1bufs.ap(0, 128, c * D + D2, D2),
                            AF.Square, bias=0.0, scale=1.0),
                            waits=[xa, sh_free[ph]])
                        r1b = vop(lambda v, c=c, ph=ph: v.tensor_reduce(
                            stt.ap(0, 128, 2 * NHC + c, 1),
                            scrf.ap(0, 128, ph * D2, D2),
                            AX.X, ALU.add), waits=[sq])
                        sh_free[ph] = r1b
                        vop(lambda v, c=c: v.tensor_tensor(
                            stt.ap(0, 128, NHC + c, 1),
                            stt.ap(0, 128, NHC + c, 1),
                            stt.ap(0, 128, 2 * NHC + c, 1), ALU.add))
                bPAT = phase_barrier()
                AR.freebuf(wob2, toks=bPAT)
            AR.freebuf(ofm, toks=bPAT)
            w1buf = AR.alloc("w1b", [128, KC * D2], BF16)
            tw12_0 = load_w_into(w1buf, w1, D, D, col0=0, ncols=D2,
                                 waits=list(w1buf.birth))
            if half == 0:
                pre_xmf[0] = AR.alloc("xmf", [128, NHC * KC * 128], BF16)
            else:
                v2tb, tV2 = load_w("v2tb", v2t, D, KR, side="right")
                b1_sb = AR.alloc("b1sb", [128, D], F32, side="right")
                tB1 = cdma(b1_sb[:, :], b1b, waits=b1_sb.birth,
                           sem=next_wsem())
                w2b, tW2 = load_w("w2b", w2, D, KE, side="right")
                utb, tUT = load_w("utb", ut, KR, D, side="right")
            scrf_free = sh_free[0]
            vop(lambda v: v.tensor_scalar(stt.ap(0, 128, 0, 2 * NHC),
                                          stt.ap(0, 128, 0, 2 * NHC),
                                          1.0 / D, None, ALU.mult))
            vop(lambda v: v.tensor_tensor(stt.ap(0, 128, 2 * NHC, NHC),
                                          stt.ap(0, 128, 0, NHC),
                                          stt.ap(0, 128, 0, NHC), ALU.mult))
            vop(lambda v: v.tensor_tensor(stt.ap(0, 128, NHC, NHC),
                                          stt.ap(0, 128, NHC, NHC),
                                          stt.ap(0, 128, 2 * NHC, NHC),
                                          ALU.subtract))
            s_i = sop(lambda s: s.activation(stt.ap(0, 128, NHC, NHC),
                                             stt.ap(0, 128, NHC, NHC),
                                             AF.Sqrt, bias=EPS, scale=1.0),
                      waits=[P.cur("VE")])
            vop(lambda v: v.reciprocal(stt.ap(0, 128, NHC, NHC),
                                       stt.ap(0, 128, NHC, NHC)), waits=[s_i])
            vop(lambda v: v.tensor_tensor(stt.ap(0, 128, 3 * NHC, NHC),
                                          stt.ap(0, 128, 0, NHC),
                                          stt.ap(0, 128, NHC, NHC), ALU.mult))
            nb = vop(lambda v: v.tensor_scalar(stt.ap(0, 128, 3 * NHC, NHC),
                                               stt.ap(0, 128, 3 * NHC, NHC),
                                               -1.0, None, ALU.mult))
            x2fm = AR.alloc("x2fm", [128, KC * TH], BF16)
            x2fm_tok = [None] * NHC
            for c in range(NHC):
                xn = sop(lambda s, c=c: s.activation(
                    x1bufs.ap(0, 128, c * D, D), x1bufs.ap(0, 128, c * D, D),
                    AF.Identity, bias=stt.h[0:128, 3 * NHC + c:3 * NHC + c + 1],
                    scale=stt.h[0:128, NHC + c:NHC + c + 1]), waits=[nb])
                xb = vop(lambda v, c=c: v.tensor_copy(
                    scrb.ap(0, 128, 0, D), x1bufs.ap(0, 128, c * D, D)),
                    waits=[xn, scrb_free] + scrb.birth)
                for k0 in range(0, KC, 4):
                    nk = min(4, KC - k0)
                    bnk = TBK[k0 // 4 % 2]
                    mt = None
                    for j in range(nk):
                        mt = mm(bnk[0:128, j * 128:(j + 1) * 128],
                                scrb.ap(0, 128, (k0 + j) * 128, 128),
                                idb_sb.h[0:128, 0:128],
                                start=(j == 0), stop=(j == nk - 1),
                                waits=[xb, t_idb, bfree(bnk)] if j == 0 else (),
                                inc=(j == nk - 1), tp=True)
                    evt = vop(lambda v, b=bnk, nk=nk, c=c, k0=k0: v.tensor_copy(
                        x2fm.ap3(0, 128, k0 * TH + c * 128, [[TH, nk], [1, 128]]),
                        b[0:128, 0:nk * 128]), waits=[mt] + x2fm.birth)
                    bset(bnk, evt)
                    scrb_free = P.cur("MM")
                    x2fm_tok[c] = evt
            bWo = phase_barrier()

            if half == 0:
                v2tb, tV2 = load_w("v2tb", v2t, D, KR, side="right")
                b1_sb = AR.alloc("b1sb", [128, D], F32, side="right")
                tB1 = cdma(b1_sb[:, :], b1b, waits=b1_sb.birth,
                           sem=next_wsem())
                w2b, tW2 = load_w("w2b", w2, D, KE, side="right")
                utb, tUT = load_w("utb", ut, KR, D, side="right")
            hfm = AR.alloc("hfm", [128, KC * 128], BF16)
            vxsb = AR.alloc("vxsb", [128, NHC * KR], BF16)
            alscb = [AR.alloc("alsc", [128, 2 * KE], F32) for _ in range(2)]
            vxwb = AR.alloc("vxwb", [128, KR], BF16)
            vxwf = AR.alloc("vxwf", [128, KRC * 128], BF16)
            st2 = AR.alloc("st2", [128, 4 * NHC], F32)

            mal = None
            alsc_free = [None, None]
            vxwb_free = None
            ts_pend = [None]

            def ts_mal(c, wh, hr):
                nonlocal mal, scrb_free
                HC2 = D2 // 128
                tts = []
                for k0 in range(0, HC2, 4):
                    nk = min(4, HC2 - k0)
                    bnk = TBK[k0 // 4 % 2]
                    mt = None
                    for j in range(nk):
                        mt = mm(bnk[0:128, j * 128:(j + 1) * 128],
                                scrb.ap(0, 128, (k0 + j) * 128, 128),
                                idb_sb.h[0:128, 0:128],
                                start=(j == 0), stop=(j == nk - 1),
                                waits=[hr, t_idb, bfree(bnk)] if j == 0 else (),
                                inc=(j == nk - 1), tp=True)
                    evt = vop(lambda v, b=bnk, nk=nk, k0=k0: v.tensor_copy(
                        hfm.ap(0, 128, k0 * 128, nk * 128),
                        b[0:128, 0:nk * 128]), waits=[mt] + hfm.birth)
                    bset(bnk, evt)
                    tts.append(evt)
                    scrb_free = P.cur("MM")
                for k2 in range(HC2):
                    first = (wh == 0 and c == 0 and k2 == 0)
                    lastg = (wh == 1 and c == NHC - 1 and k2 == HC2 - 1)
                    mal = mm(FB[5][0:128, c * KE:(c + 1) * KE],
                             hfm.ap(0, 128, k2 * 128, 128),
                             w2b.ap(0, 128, (wh * HC2 + k2) * KE, KE),
                             start=first, stop=lastg,
                             waits=([tW2(-1), bfree(FB[5])] + tts) if first
                             else tts,
                             inc=lastg)

            for wh in range(2):
                tw12 = (tw12_0 if wh == 0 else
                        load_w_into(w1buf, w1, D, D, col0=D2, ncols=D2,
                                    waits=[P.cur("MM")]))
                w1b2 = w1buf
                HC2 = D2 // 128
                for c in range(NHC):
                    bk = [FB[0], FB[1]] if c % 2 == 0 else [FB[2], FB[3]]
                    x2sl = (lambda k, c=c: x2fm.ap(0, 128, k * TH + c * 128, 128))
                    mH = gemm(x2sl,
                              lambda k, g: w1b2.ap(0, 128, k * D2 + g * 512,
                                                   min(512, D2 - g * 512)),
                              KC, bk, D2, waits=[x2fm_tok[c]], ktok=tw12)
                    if wh == 0:
                        mvx = None
                        for k in range(KC):
                            mvx = mm(FB[4][0:128, 0:min(512, KR)],
                                     x2fm.ap(0, 128, k * TH + c * 128, 128),
                                     v2tb.ap(0, 128, k * KR, min(512, KR)),
                                     start=(k == 0), stop=(k == KC - 1),
                                     waits=[tV2(k), bfree(FB[4]), x2fm_tok[c]]
                                     if k == 0 else [tV2(k)],
                                     inc=(k == KC - 1))
                        vxe = vop(lambda v, c=c: v.tensor_copy(
                            vxsb.ap(0, 128, c * KR, KR),
                            FB[4][0:128, 0:KR]), waits=[mvx] + vxsb.birth)
                        bset(FB[4], vxe)
                    if ts_pend[0] is not None:
                        ts_pend[0]()
                        ts_pend[0] = None
                    ha = None
                    for g in range((D2 + 511) // 512):
                        cn = min(512, D2 - g * 512)
                        ha = vop(lambda v, g=g, cn=cn, wh=wh, bk=bk: v.tensor_tensor(
                            scrf.ap(0, 128, g * 512, cn), bk[g][0:128, 0:cn],
                            b1_sb.h[0:128, wh * D2 + g * 512:wh * D2 + g * 512 + cn],
                            ALU.add), waits=[mH, tB1, scrf_free])
                    for g in range((D2 + 511) // 512):
                        bset(bk[g], ha)
                    hr = sop(lambda s: s.activation(
                        scrb.ap(0, 128, 0, D2), scrf.ap(0, 128, 0, D2), AF.Relu,
                        bias=0.0, scale=1.0), waits=[ha, scrb_free])
                    scrf_free = hr
                    ts_pend[0] = (lambda c=c, wh=wh, hr=hr: ts_mal(c, wh, hr))
            ts_pend[0]()
            ts_pend[0] = None
            bWo = phase_barrier()
            AR.freebuf(w1buf, toks=phase_barrier())
            if half == 0:
                ws_pre[0] = load_w("wsb2", ws, D, MD)
            mo_tok = [None] * NHC

            def ffn_x3_adds(c):
                x3 = None
                for g in range((D + 511) // 512):
                    cn = min(512, D - g * 512)
                    x3 = vop(lambda v, c=c, g=g, cn=cn: v.tensor_tensor(
                        x1bufs.ap(0, 128, c * D + g * 512, cn),
                        x1bufs.ap(0, 128, c * D + g * 512, cn),
                        FB[g][0:128, 0:cn], ALU.add), waits=[mo_tok[c]])
                for g in range((D + 511) // 512):
                    bset(FB[g], x3)
                return x3

            def ffn_x3_stats(c, x3):
                vop(lambda v, c=c: v.tensor_reduce(
                    st2.ap(0, 128, c, 1), x1bufs.ap(0, 128, c * D, D), AX.X,
                    ALU.add), waits=[x3] + st2.birth)
                sq = sop(lambda s, c=c: s.activation(
                    scrf.ap(0, 128, 0, D), x1bufs.ap(0, 128, c * D, D),
                    AF.Square, bias=0.0, scale=1.0), waits=[x3, scrf_free])
                return vop(lambda v, c=c: v.tensor_reduce(
                    st2.ap(0, 128, NHC + c, 1), scrf.ap(0, 128, 0, D), AX.X,
                    ALU.add), waits=[sq])

            def ffn_alpha(c):
                alsc = alscb[c % 2]
                al = vop(lambda v, c=c, alsc=alsc: v.tensor_tensor(
                    alsc.ap(0, 128, 0, KE), FB[5][0:128, c * KE:(c + 1) * KE],
                    b2_sb.h[0:128, 0:KE], ALU.add),
                    waits=[mal, t_b2, alsc_free[c % 2]] + alsc.birth)
                vop(lambda v, alsc=alsc: v.tensor_reduce(
                    alsc.ap(0, 128, KE, 1), alsc.ap(0, 128, 0, KE), AX.X,
                    ALU.max, negate=True))
                vop(lambda v, alsc=alsc: v.tensor_tensor(
                    alsc.ap(0, 128, 0, KE), alsc.ap(0, 128, 0, KE),
                    alsc.ap3(0, 128, KE, [[0, KE]]), ALU.add))
                ae = sop(lambda s, alsc=alsc: s.activation(
                    alsc.ap(0, 128, KE, KE), alsc.ap(0, 128, 0, KE), AF.Exp,
                    bias=0.0, scale=1.0), waits=[P.cur("VE")])
                vop(lambda v, alsc=alsc: v.tensor_reduce(
                    alsc.ap(0, 128, 0, 1), alsc.ap(0, 128, KE, KE), AX.X,
                    ALU.add), waits=[ae])
                vop(lambda v, alsc=alsc: v.reciprocal(alsc.ap(0, 128, 0, 1),
                                                      alsc.ap(0, 128, 0, 1)))
                av = vop(lambda v, alsc=alsc: v.tensor_tensor(
                    alsc.ap(0, 128, KE, KE), alsc.ap(0, 128, KE, KE),
                    alsc.ap3(0, 128, 0, [[0, KE]]), ALU.mult))
                alsc_free[c % 2] = ae
                return av

            av_tok = [None] * NHC
            av_tok[0] = ffn_alpha(0)
            x3a = None
            for c in range(NHC):
                alsc = alscb[c % 2]
                vw = vop(lambda v, c=c, alsc=alsc: v.tensor_tensor(
                    vxwb.ap(0, 128, 0, KR), vxsb.ap(0, 128, c * KR, KR),
                    alsc.ap3(0, 128, KE, [[1, KE], [0, R]]), ALU.mult),
                    waits=[av_tok[c], vxwb_free] + vxwb.birth)
                if c + 1 < NHC:
                    av_tok[c + 1] = ffn_alpha(c + 1)
                if c >= 1:
                    x3a = ffn_x3_adds(c - 1)
                for k0 in range(0, KRC, 4):
                    nk = min(4, KRC - k0)
                    bnk = TBK[k0 // 4 % 2]
                    mt = None
                    for j in range(nk):
                        mt = mm(bnk[0:128, j * 128:(j + 1) * 128],
                                vxwb.ap(0, 128, (k0 + j) * 128, 128),
                                idb_sb.h[0:128, 0:128],
                                start=(j == 0), stop=(j == nk - 1),
                                waits=[vw, t_idb, bfree(bnk)] if j == 0 else (),
                                inc=(j == nk - 1), tp=True)
                    evt = sop(lambda s, b=bnk, nk=nk, k0=k0: s.copy(
                        vxwf.ap(0, 128, k0 * 128, nk * 128), b[0:128, 0:nk * 128]),
                        waits=[mt] + vxwf.birth)
                    bset(bnk, evt)
                    vxwb_free = P.cur("MM")
                tvw = P.cur("SC")
                mo = gemm(lambda kr: vxwf.ap(0, 128, kr * 128, 128),
                          lambda kr, g: utb.ap(0, 128, kr * D + g * 512,
                                               min(512, D - g * 512)),
                          KRC, [FB[0], FB[1], FB[2], FB[3]], D,
                          waits=[tvw], ktok=tUT)
                mo_tok[c] = mo
                if c >= 1:
                    scrf_free = ffn_x3_stats(c - 1, x3a)
            scrf_free = ffn_x3_stats(NHC - 1, ffn_x3_adds(NHC - 1))
            vop(lambda v: v.tensor_scalar(st2.ap(0, 128, 0, 2 * NHC),
                                          st2.ap(0, 128, 0, 2 * NHC),
                                          1.0 / D, None, ALU.mult),
                waits=[scrf_free])
            vop(lambda v: v.tensor_tensor(st2.ap(0, 128, 2 * NHC, NHC),
                                          st2.ap(0, 128, 0, NHC),
                                          st2.ap(0, 128, 0, NHC), ALU.mult))
            vop(lambda v: v.tensor_tensor(st2.ap(0, 128, NHC, NHC),
                                          st2.ap(0, 128, NHC, NHC),
                                          st2.ap(0, 128, 2 * NHC, NHC),
                                          ALU.subtract))
            s_j = sop(lambda s: s.activation(st2.ap(0, 128, NHC, NHC),
                                             st2.ap(0, 128, NHC, NHC),
                                             AF.Sqrt, bias=EPS, scale=1.0),
                      waits=[P.cur("VE")])
            vop(lambda v: v.reciprocal(st2.ap(0, 128, NHC, NHC),
                                       st2.ap(0, 128, NHC, NHC)), waits=[s_j])
            vop(lambda v: v.tensor_tensor(st2.ap(0, 128, 3 * NHC, NHC),
                                          st2.ap(0, 128, 0, NHC),
                                          st2.ap(0, 128, NHC, NHC), ALU.mult))
            nb2 = vop(lambda v: v.tensor_scalar(st2.ap(0, 128, 3 * NHC, NHC),
                                                st2.ap(0, 128, 3 * NHC, NHC),
                                                -1.0, None, ALU.mult))
            for c in range(NHC):
                ob = c % 2
                on = sop(lambda s, c=c: s.activation(
                    x1bufs.ap(0, 128, c * D, D), x1bufs.ap(0, 128, c * D, D),
                    AF.Identity, bias=st2.h[0:128, 3 * NHC + c:3 * NHC + c + 1],
                    scale=st2.h[0:128, NHC + c:NHC + c + 1]),
                    waits=[nb2])
                sdma(yout[tok0 + c * 128:tok0 + c * 128 + 128, :],
                     x1bufs.ap(0, 128, c * D, D), waits=[on], sem=f"OD{ob}")
            bFFN = phase_barrier() + [P.cur("OD0"), P.cur("OD1")]
            for b in (st2, vxwf, vxwb, alscb[0], alscb[1], vxsb, hfm,
                      x2fm, scrb, scrf, stt, x1bufs):
                AR.freebuf(b, toks=bFFN)
            if half == 0:
                for b in (utb, v2tb, w2b, b1_sb):
                    AR.freebuf(b, toks=bFFN)
                h1_pre[0] = load_w("wkb2", wk, D, QD)
                h1_pre[1] = load_w("wvb2", wv, D, VD, col0=0, ncols=GV)

        P.wait("scalar", P.cur("OD0"))
        P.wait("scalar", P.cur("OD1"))
        P.wait("sync", P.cur("OD0"))
        P.wait("sync", P.cur("OD1"))

        with nc.Block() as block:
            @block.sync
            def _(eng):
                for fn in P.streams["sync"]:
                    fn(eng)

            @block.gpsimd
            def _(eng):
                for fn in P.streams["gpsimd"]:
                    fn(eng)

            @block.tensor
            def _(eng):
                for fn in P.streams["tensor"]:
                    fn(eng)

            @block.vector
            def _(eng):
                for fn in P.streams["vector"]:
                    fn(eng)

            @block.scalar
            def _(eng):
                for fn in P.streams["scalar"]:
                    fn(eng)

        build_stats = dict(ninst=dict(P.ninst), arena_peak=AR.peak)
        nc._mk_build_stats = build_stats
    return nc


def make_inputs(cfg, full, core):
    D, H, TP, TM = cfg["D"], cfg["H"], cfg["TP"], cfg["TM"]
    DK, DV, M, KE, R = cfg["DK"], cfg["DV"], cfg["M"], cfg["KE"], cfg["R"]
    KR = KE * R
    seq, halfid = core // 2, core % 2
    x = full["x"]
    xm = np.ascontiguousarray(x[seq, halfid * TM:(halfid + 1) * TM, :])
    if halfid == 1:
        xp = np.ascontiguousarray(x[seq, 0:TP, :])
        pb = np.zeros((128, 1), np.float32)
    else:
        xp = np.zeros((TP, D), np.float32)
        pb = np.full((128, 1), -1e38, np.float32)
    ones = np.ones
    f = np.float32
    return dict(
        x_main=xm, x_pre=xp,
        wq=full["wq"], wk=full["wk"], ws=full["ws"], wv=full["wv"],
        wg=full["wg"], wo=full["wo"], w1=full["w1"], w2=full["w2"],
        ut=full["ut"], v2t=full["v2t"],
        b1b=np.ascontiguousarray(np.broadcast_to(full["b1"], (128, D))).astype(f),
        b2b=np.ascontiguousarray(np.broadcast_to(full["b2"], (128, KE))).astype(f),
        gnwb=np.ascontiguousarray(np.broadcast_to(full["gnw"], (128, DV))).astype(f),
        csts=np.ascontiguousarray(np.broadcast_to(
            np.array([0.0, 1.0, EPS, 1.0 / DV, 0, 0, 0, 0], f), (128, 8))),
        pbias=pb,
        identf=np.eye(128, dtype=f),
        lmaskf=np.triu(np.ones((128, 128), f)),
        onescol=ones((128, 1), f),
        onesrow=ones((1, 128), f),
    )


def kernel(L0_attn_hidden_states, L0_attn_Wq, L0_attn_Wk, L0_attn_Wv,
           L0_attn_Wg, L0_attn_Ws, L0_attn_Wo, L0_attn_g_norm_weight,
           L0_ffn_U, L0_ffn_V, L0_ffn_W1, L0_ffn_b1, L0_ffn_W2, L0_ffn_b2):
    cfg = dict(DEF_CFG)
    D, KE, R = cfg["D"], cfg["KE"], cfg["R"]
    full = dict(
        x=np.ascontiguousarray(L0_attn_hidden_states),
        wq=np.ascontiguousarray(L0_attn_Wq), wk=np.ascontiguousarray(L0_attn_Wk),
        ws=np.ascontiguousarray(L0_attn_Ws), wv=np.ascontiguousarray(L0_attn_Wv),
        wg=np.ascontiguousarray(L0_attn_Wg), wo=np.ascontiguousarray(L0_attn_Wo),
        w1=np.ascontiguousarray(L0_ffn_W1), w2=np.ascontiguousarray(L0_ffn_W2),
        ut=np.ascontiguousarray(L0_ffn_U.transpose(0, 2, 1).reshape(KE * R, D)),
        v2t=np.ascontiguousarray(L0_ffn_V.transpose(2, 0, 1).reshape(D, KE * R)),
        b1=np.asarray(L0_ffn_b1), b2=np.asarray(L0_ffn_b2),
        gnw=np.asarray(L0_attn_g_norm_weight),
    )
    nc = build_program(cfg)
    in_maps = [make_inputs(cfg, full, c) for c in range(8)]
    res = run_bass_kernel_spmd(nc, in_maps, core_ids=list(range(8)))
    B, T = L0_attn_hidden_states.shape[0], L0_attn_hidden_states.shape[1]
    out = np.empty((B, T, D), np.float32)
    TM = cfg["TM"]
    for c in range(8):
        seq, halfid = c // 2, c % 2
        out[seq, halfid * TM:(halfid + 1) * TM, :] = res.results[c]["y"]
    return out


# version 39
# speedup vs baseline: 1.0066x; 1.0066x over previous
import numpy as np
from contextlib import ExitStack

import concourse.bass as bass
import concourse.mybir as mybir
from concourse.bass_utils import run_bass_kernel_spmd
from concourse.alu_op_type import AluOpType as ALU

F32 = mybir.dt.float32
BF16 = mybir.dt.bfloat16
AF = mybir.ActivationFunctionType
AX = mybir.AxisListType

ENGINES = ("sync", "gpsimd", "tensor", "vector", "scalar")

DEF_CFG = dict(D=2048, H=16, TP=1024, TM=1024, DK=64, DV=128, M=64, KE=8, R=64)
EPS = 1e-5
CLM, CLX = -32.0, 32.0
QSC = 0.125


class _Sem:
    __slots__ = ("h", "n")

    def __init__(self, h):
        self.h = h
        self.n = 0


class Prog:
    def __init__(self, nc):
        self.nc = nc
        self.streams = {e: [] for e in ENGINES}
        self.sems = {}
        self.hw = {e: {} for e in ENGINES}
        self.ninst = {e: 0 for e in ENGINES}

    def add_sem(self, name, handle):
        self.sems[name] = _Sem(handle)

    ONESHOT = ("SYI", "GPI")

    def wait(self, eng, tok):
        if tok is None:
            return
        sem, n = tok
        if sem in self.ONESHOT and n > 0:
            n = self.sems[sem].n
        if n <= 0 or self.hw[eng].get(sem, 0) >= n:
            return
        self.hw[eng][sem] = n
        h = self.sems[sem].h
        self.streams[eng].append(lambda e, h=h, n=n: e.wait_ge(h, n))
        self.ninst[eng] += 1

    def waits(self, eng, toks):
        for t in toks:
            self.wait(eng, t)

    def run(self, eng, fn):
        self.streams[eng].append(fn)
        self.ninst[eng] += 1

    def inc(self, eng, sem, fn, dn=1):
        s = self.sems[sem]
        s.n += dn
        h = s.h
        self.streams[eng].append(lambda e, fn=fn, h=h, dn=dn: fn(e).then_inc(h, dn))
        self.ninst[eng] += 1
        return (sem, s.n)

    def cur(self, sem):
        return (sem, self.sems[sem].n)


class Buf:
    __slots__ = ("h", "shape", "dtype", "birth", "nm", "pitch")

    def __init__(self, h, shape, dtype, birth, nm):
        self.h = h
        self.shape = list(shape)
        self.dtype = dtype
        self.birth = list(birth)
        self.nm = nm
        self.pitch = int(np.prod(shape[1:]))

    def __getitem__(self, sl):
        return self.h[sl]

    def ap(self, p0, np_, c0, nc_):
        return self.h[p0:p0 + np_, c0:c0 + nc_]

    def ap3(self, p0, np_, c0, pairs):
        return bass.AP(self.h, p0 * self.pitch + c0,
                       [[self.pitch, np_]] + [list(p) for p in pairs])


class Arena:

    def __init__(self, nc, nbytes):
        self.nc = nc
        self.nbytes = nbytes
        self._cm = nc.sbuf_tensor("arena", [128, nbytes], mybir.dt.uint8)
        self.handle = self._cm.__enter__()
        self.baseaddr = nc.lookup_mloc(self.handle).addr
        self.free = [[0, nbytes, [], 0]]
        self.live = {}
        self.idx = 0
        self.stamp = 0
        self.peak = 0

    def alloc(self, name, shape, dtype, toks=(), side="left"):
        per_part = int(np.prod(shape[1:])) * mybir.dt.size(dtype)
        per_part = (per_part + 31) // 32 * 32
        if side == "left" and per_part <= 16384:
            side = "right"
        cand = [g for g in self.free if g[1] - g[0] >= per_part]
        if side == "right":
            pick = [cand[-1]] if cand else []
        else:
            old = [g for g in cand if g[3] <= self.stamp - 6]
            pick = old if old else cand
        for seg in pick:
            if True:
                birth = list(seg[2]) + list(toks)
                if side == "right":
                    off = seg[1] - per_part
                    seg[1] -= per_part
                else:
                    off = seg[0]
                    seg[0] += per_part
                if seg[0] >= seg[1]:
                    self.free.remove(seg)
                self.idx += 1
                nm = f"{name}_{self.idx}"
                h = self.nc.alloc_sbuf_tensor_at(nm, list(shape), dtype,
                                                 offset=self.baseaddr + off)
                self.live[nm] = (off, per_part)
                inuse = self.nbytes - sum(s[1] - s[0] for s in self.free)
                self.peak = max(self.peak, inuse)
                return Buf(h, shape, dtype, birth, nm)
        live = sorted(((off, sz, nm) for nm, (off, sz) in self.live.items()))
        raise MemoryError(f"arena OOM for {name}: need {per_part}B/part, "
                          f"free={[(s[0], s[1]) for s in self.free]}, "
                          f"live={live}")

    def freebuf(self, buf, toks=()):
        off, sz = self.live.pop(buf.nm)
        self.stamp += 1
        self.free.append([off, off + sz, list(toks), self.stamp])
        self.free.sort(key=lambda s: s[0])
        merged = []
        for seg in self.free:
            if merged and merged[-1][1] == seg[0]:
                merged[-1][1] = seg[1]
                merged[-1][2] = merged[-1][2] + seg[2]
                merged[-1][3] = max(merged[-1][3], seg[3])
            else:
                merged.append(seg)
        self.free = merged


def build_program(cfg):
    D, H, TP, TM = cfg["D"], cfg["H"], cfg["TP"], cfg["TM"]
    DK, DV, M, KE, R = cfg["DK"], cfg["DV"], cfg["M"], cfg["KE"], cfg["R"]
    KC = D // 128
    VD = H * DV
    VDC = VD // 128
    QD = H * DK
    MD = H * M
    NPC = TP // 128
    TH = TM // 2
    NHC = TH // 128
    GH = 4 if H % 4 == 0 else H
    NG = H // GH
    KR = KE * R
    KRC = KR // 128
    D2 = D // 2
    GV = VD // 2
    HH = H // 2

    nc = bass.Bass("TRN2", target_bir_lowering=False)

    def dri(name, shape):
        return nc.dram_tensor(name, list(shape), F32, kind="ExternalInput").ap()

    x_main = dri("x_main", [TM, D])
    x_pre = dri("x_pre", [TP, D])
    wq = dri("wq", [D, QD]); wk = dri("wk", [D, QD]); ws = dri("ws", [D, MD])
    wv = dri("wv", [D, VD]); wg = dri("wg", [D, VD]); wo = dri("wo", [VD, D])
    w1 = dri("w1", [D, D]); w2 = dri("w2", [D, KE])
    ut = dri("ut", [KR, D]); v2t = dri("v2t", [D, KR])
    b1b = dri("b1b", [128, D]); b2b = dri("b2b", [128, KE])
    gnwb = dri("gnwb", [128, DV]); csts = dri("csts", [128, 8])
    pbias = dri("pbias", [128, 1])
    identf = dri("identf", [128, 128]); lmaskf = dri("lmaskf", [128, 128])
    onescol = dri("onescol", [128, 1]); onesrow = dri("onesrow", [1, 128])
    yout = nc.dram_tensor("y", [TM, D], F32, kind="ExternalOutput").ap()

    ctx = ExitStack()
    with ctx:
        FB = [ctx.enter_context(nc.psum_tensor(f"fb{i}", [128, 512], F32))
              for i in range(6)]
        TBK = [ctx.enter_context(nc.psum_tensor(f"tb{i}", [128, 1024], BF16))
               for i in range(2)]

        P = Prog(nc)
        for s in (("SYI", "GPI", "GPC", "MM", "VE", "SC", "ST0", "ST1",
                   "OD0", "OD1")
                  + tuple(f"W{i}" for i in range(8))):
            P.add_sem(s, ctx.enter_context(nc.semaphore(s)))
        wsem_rot = [0]

        def next_wsem():
            nm = f"W{wsem_rot[0] % 8}"
            wsem_rot[0] += 1
            return nm

        AR = Arena(nc, 206 * 1024)

        def ldma(out_ap, in_ap, waits=(), sem="SYI", quiesce=None):
            P.waits("sync", waits)
            if (quiesce is None and sem not in Prog.ONESHOT) or quiesce:
                P.wait("sync", (sem, P.sems[sem].n))
            return P.inc("sync", sem,
                         lambda e, o=out_ap, i=in_ap: e.dma_start(out=o, in_=i), 16)

        def cdma(out_ap, in_ap, waits=(), sem="GPI", quiesce=None):
            P.waits("gpsimd", waits)
            if (quiesce is None and sem not in Prog.ONESHOT) or quiesce:
                P.wait("gpsimd", (sem, P.sems[sem].n))
            return P.inc("gpsimd", sem,
                         lambda e, o=out_ap, i=in_ap: e.dma_start(out=o, in_=i), 16)

        def sdma(out_ap, in_ap, waits=(), sem="OD0", quiesce=None):
            P.waits("scalar", waits)
            if (quiesce is None and sem not in Prog.ONESHOT) or quiesce:
                P.wait("scalar", (sem, P.sems[sem].n))
            return P.inc("scalar", sem,
                         lambda e, o=out_ap, i=in_ap: e.dma_start(out=o, in_=i), 16)

        def mm(out_ap, lhsT, rhs, start, stop, waits=(), inc=False, tp=False):
            P.waits("tensor", waits)
            fn = (lambda t, o=out_ap, l=lhsT, r=rhs, a=start, b=stop, tpp=tp:
                  t.matmul(o, l, r, start=a, stop=b, is_transpose=(True if tpp else None)))
            if inc:
                return P.inc("tensor", "MM", fn)
            P.run("tensor", fn)
            return None

        class _Rec:

            def __init__(self):
                self.call = None

            def __getattr__(self, name):
                def f(*a, **k):
                    self.call = (name, a, k)
                return f

        def _eager(fn):
            r = _Rec()
            fn(r)
            name, a, k = r.call
            return lambda e, n=name, a=a, k=k: getattr(e, n)(*a, **k)

        def vop(fn, waits=()):
            P.waits("vector", waits)
            P.wait("vector", ("VE", P.sems["VE"].n))
            return P.inc("vector", "VE", _eager(fn))

        def sop(fn, waits=()):
            P.waits("scalar", waits)
            P.wait("scalar", ("SC", P.sems["SC"].n))
            return P.inc("scalar", "SC", _eager(fn))

        def gop(fn, waits=()):
            P.waits("gpsimd", waits)
            P.wait("gpsimd", ("GPC", P.sems["GPC"].n))
            return P.inc("gpsimd", "GPC", _eager(fn))

        bank_free = {}

        def bfree(bank):
            return bank_free.get(id(bank))

        def bset(bank, tok):
            bank_free[id(bank)] = tok

        csb = AR.alloc("csb", [128, 8], F32, side="right")
        pb_sb = AR.alloc("pb", [128, 1], F32, side="right")
        idf_sb = AR.alloc("idf", [128, 128], F32, side="right")
        idb_sb = AR.alloc("idb", [128, 128], BF16, side="right")
        lmf_sb = AR.alloc("lmf", [128, 128], F32, side="right")
        lmb_sb = AR.alloc("lmb", [128, 128], BF16, side="right")
        oc_sb = AR.alloc("ocol", [128, 1], BF16, side="right")
        or_sb = AR.alloc("orow", [1, 128], BF16, side="right")
        gnw_sb = AR.alloc("gnw", [128, DV], F32, side="right")
        b2_sb = AR.alloc("b2", [128, KE], F32, side="right")

        for v, i in ((0.0, 0), (1.0, 1), (EPS, 2), (1.0 / DV, 3)):
            nc.const_aps.register(dtype=F32, value=v, ap=csb.h[0:128, i:i + 1])

        t_cs = ldma(csb[:, :], csts)
        t_pb = ldma(pb_sb[:, :], pbias)
        t_idf = ldma(idf_sb[:, :], identf)
        t_lmf = ldma(lmf_sb[:, :], lmaskf)
        t_gnw = ldma(gnw_sb[:, :], gnwb)
        t_b2 = ldma(b2_sb[:, :], b2b)
        t_idb = cdma(idb_sb[:, :], identf)
        t_lmb = cdma(lmb_sb[:, :], lmaskf)
        t_oc = cdma(oc_sb[:, :], onescol)
        t_or = cdma(or_sb[:, :], onesrow)
        P.wait("scalar", t_cs)

        s1f = AR.alloc("s1f", [64, H * M], F32, side="right")
        s2f = AR.alloc("s2f", [64, H * DV], F32, side="right")
        psf = AR.alloc("psf", [1, H * M], F32, side="right")
        s1b = AR.alloc("s1b", [64, H * M], BF16, side="right")
        s2b = AR.alloc("s2b", [64, H * DV], BF16, side="right")
        psb = AR.alloc("psb", [1, H * M], BF16, side="right")
        zt = None
        for st, pr in ((s1f, 64), (s2f, 64), (psf, 1), (s1b, 64), (s2b, 64), (psb, 1)):
            zt = vop(lambda v, a=st, pr=pr: v.memset(a.h[0:pr, :], 0))
        s1b_rdy = [zt] * NG
        s2b_rdy = [zt] * NG
        psb_rdy = [zt] * NG
        s1f_rdy = [zt] * NG
        s2f_rdy = [zt] * NG
        psf_rdy = [zt] * NG

        def load_w(name, dram, rows, cols, col0=0, ncols=None, per_dma=4,
                   side="left"):
            w = cols if ncols is None else ncols
            rc = rows // 128
            buf = AR.alloc(name, [128, rc * w], BF16, side=side)
            toks = list(buf.birth)
            npc = (rc + per_dma - 1) // per_dma
            sems = [next_wsem() for _ in range(min(npc, 4))]
            ctoks = {}
            for i, k0 in enumerate(range(0, rc, per_dma)):
                nk = min(per_dma, rc - k0)
                dap = bass.AP(dram.tensor, dram.offset + (k0 * 128) * cols + col0,
                              [[cols, 128], [128 * cols, nk], [1, w]])
                sap = bass.AP(buf.h, k0 * w, [[rc * w, 128], [w, nk], [1, w]])
                t = cdma(sap, dap, waits=toks, sem=sems[i % len(sems)],
                         quiesce=True)
                toks = []
                for k in range(k0, k0 + nk):
                    ctoks[k] = t
            last = ctoks[rc - 1]

            def tok_of(k=-1, ctoks=ctoks, last=last):
                return last if k < 0 else ctoks[min(k, rc - 1)]
            return buf, tok_of

        def load_w_into(buf, dram, rows, cols, col0=0, ncols=None,
                        per_dma=4, waits=()):
            w = cols if ncols is None else ncols
            rc = rows // 128
            toks = list(waits)
            npc = (rc + per_dma - 1) // per_dma
            sems = [next_wsem() for _ in range(min(npc, 4))]
            ctoks = {}
            for i, k0 in enumerate(range(0, rc, per_dma)):
                nk = min(per_dma, rc - k0)
                dap = bass.AP(dram.tensor, dram.offset + (k0 * 128) * cols + col0,
                              [[cols, 128], [128 * cols, nk], [1, w]])
                sap = bass.AP(buf.h, k0 * w, [[buf.pitch, 128], [w, nk], [1, w]])
                t = cdma(sap, dap, waits=toks, sem=sems[i % len(sems)],
                         quiesce=True)
                toks = []
                for k in range(k0, k0 + nk):
                    ctoks[k] = t
            last = ctoks[rc - 1]

            def tok_of(k=-1, ctoks=ctoks, last=last):
                return last if k < 0 else ctoks[min(k, rc - 1)]
            return tok_of

        def gemm(lhs_fn, rhs_fn, nk, banks, ncols, waits=(), ktok=None):
            ngrp = (ncols + 511) // 512
            w0 = list(waits) + [bfree(b) for b in banks[:ngrp]]
            mt = None
            for k in range(nk):
                if ktok is not None:
                    P.wait("tensor", ktok(k))
                for g in range(ngrp):
                    cn = min(512, ncols - g * 512)
                    last = (k == nk - 1 and g == ngrp - 1)
                    mt = mm(banks[g][0:128, 0:cn], lhs_fn(k), rhs_fn(k, g),
                            start=(k == 0), stop=(k == nk - 1),
                            waits=w0 if (k == 0 and g == 0) else (),
                            inc=last)
            return mt

        strips = [AR.alloc(f"strip{i}", [128, D], F32, side="right") for i in range(2)]
        strip_free = [None, None]

        def strip_load(dram_x, row0, sb, waits=()):
            return ldma(strips[sb].ap(0, 128, 0, D), dram_x[row0:row0 + 128, :],
                        waits=list(waits) + [strip_free[sb]], sem=f"ST{sb}")

        def x_to_fm(sb, st_tok, dst, c0):
            ev = None
            for k0 in range(0, KC, 4):
                nk = min(4, KC - k0)
                bank = FB[4 + (k0 // 4) % 2]
                mt = None
                for j in range(nk):
                    mt = mm(bank[0:128, j * 128:(j + 1) * 128],
                            strips[sb].ap(0, 128, (k0 + j) * 128, 128),
                            idf_sb[0:128, 0:128], start=(j == 0), stop=(j == nk - 1),
                            waits=[st_tok, t_idf, bfree(bank)] if j == 0 else (),
                            inc=(j == nk - 1), tp=True)
                ev = vop(lambda v, b=bank, nk=nk, d=dst, o=c0 + k0 * 128:
                         v.tensor_copy(d.h[0:128, o:o + nk * 128], b[0:128, 0:nk * 128]),
                         waits=[mt] + dst.birth)
                bset(bank, ev)
            strip_free[sb] = P.cur("MM")
            return ev

        def tp_batch(srcs, w, bank, dst, dst_ap, waits=()):
            mt = None
            for j, sap in enumerate(srcs):
                mt = mm(bank[0:w, j * 128:(j + 1) * 128], sap, idb_sb[0:128, 0:128],
                        start=(j == 0), stop=(j == len(srcs) - 1),
                        waits=([t_idb, bfree(bank)] + list(waits)) if j == 0 else (),
                        inc=(j == len(srcs) - 1), tp=True)
            ev = vop(lambda v, b=bank, w=w, n=len(srcs), d=dst_ap:
                     v.tensor_copy(d, b[0:w, 0:n * 128]),
                     waits=[mt] + list(dst.birth))
            bset(bank, ev)
            return ev

        def phase_barrier():
            return [P.cur("MM"), P.cur("VE"), P.cur("SC"), P.cur("GPC")]

        wsb, twS = load_w("wsb", ws, D, MD, per_dma=2)
        wkb, twK = load_w("wkb", wk, D, QD, per_dma=2)
        wvb, twV = load_w("wvb", wv, D, VD, per_dma=2)

        xpf = [AR.alloc(f"xpf{i}", [128, KC * 128], BF16) for i in range(2)]
        xpf_free = [None, None]
        echkb = [AR.alloc("echk", [128, MD], BF16) for _ in range(2)]
        kchkb = [AR.alloc("kchk", [128, QD], BF16) for _ in range(2)]
        vchkb = [AR.alloc("vchk", [128, VD], BF16) for _ in range(2)]
        echk_freeb = [None, None]
        kchk_freeb = [None, None]
        vchk_freeb = [None, None]

        st = strip_load(x_pre, 0, 0)
        xe_next = x_to_fm(0, st, xpf[0], 0)
        GV2 = VD // 2
        a1_prev = {}
        ps_pend = None

        def ps_pass(echk_p, sb_p):
            for j in range((MD + 511) // 512):
                cn = min(512, MD - j * 512)
                jb = j % 2
                bk = FB[4 + jb]
                w = [t_oc, bfree(bk)] + (
                    [a1_prev[jb]] if jb in a1_prev else [])
                tok = mm(bk[0:1, 0:cn], oc_sb.h[0:128, 0:1],
                         echk_p.ap(0, 128, j * 512, cn),
                         start=True, stop=True, waits=w, inc=True)
                ap = vop(lambda v, j=j, cn=cn, bk=bk: v.tensor_tensor(
                    psf.ap(0, 1, j * 512, cn), psf.ap(0, 1, j * 512, cn),
                    bk[0:1, 0:cn], ALU.add), waits=[tok])
                for g in range(NG):
                    if (g * GH * M) // 512 == j:
                        psf_rdy[g] = ap
                bset(bk, ap)
            echk_freeb[sb_p] = P.cur("MM")

        for pc in range(NPC):
            sb = pc % 2
            echk, kchk, vchk = echkb[sb], kchkb[sb], vchkb[sb]
            xe = xe_next
            xf = xpf[sb]

            def xsl(k, xf=xf):
                return xf.ap(0, 128, k * 128, 128)

            mS_b = []
            for b in range((MD + 511) // 512):
                cn = min(512, MD - b * 512)
                mS_b.append(gemm(
                    xsl, lambda k, g, b=b, cn=cn: wsb.ap(
                        0, 128, k * MD + b * 512, cn),
                    KC, [FB[b]], cn,
                    waits=[xe, xpf_free[sb]] if b == 0 else (), ktok=twS))
            mK_b = []
            for b in range((QD + 511) // 512):
                cn = min(512, QD - b * 512)
                mK_b.append(gemm(
                    xsl, lambda k, g, b=b, cn=cn: wkb.ap(
                        0, 128, k * QD + b * 512, cn),
                    KC, [FB[2 + b]], cn, ktok=twK))
            ee = None
            for g in range((MD + 511) // 512):
                cn = min(512, MD - g * 512)
                cl = vop(lambda v, g=g, cn=cn: v.tensor_scalar(
                    FB[g][0:128, 0:cn], FB[g][0:128, 0:cn],
                    CLM, CLX, ALU.max, ALU.min), waits=[mS_b[g]])
                ee = sop(lambda s, g=g, cn=cn, echk=echk: s.activation(
                    echk.ap(0, 128, g * 512, cn), FB[g][0:128, 0:cn], AF.Exp,
                    bias=pb_sb.h[0:128, 0:1], scale=1.0),
                    waits=[cl, t_pb, echk_freeb[sb]] + echk.birth)
                bset(FB[g], ee)
            ke = None
            for g in range((QD + 511) // 512):
                cn = min(512, QD - g * 512)
                ke = sop(lambda s, g=g, cn=cn, kchk=kchk: s.copy(
                    kchk.ap(0, 128, g * 512, cn), FB[2 + g][0:128, 0:cn]),
                    waits=[mK_b[g], kchk_freeb[sb]] + kchk.birth)
                bset(FB[2 + g], ke)
            if ps_pend is not None:
                ps_pass(*ps_pend)
                ps_pend = None
            if pc + 1 < NPC:
                nsb = (pc + 1) % 2
                stn = strip_load(x_pre, (pc + 1) * 128, nsb)
                xe_next = x_to_fm(nsb, stn, xpf[nsb], 0)
            s2w = GH * DV
            s2tot = NG * s2w
            ve_tok = [None, None]
            for vh in range(2):
                bks = [FB[0], FB[1]] if vh == 0 else [FB[2], FB[3]]
                mV = gemm(xsl,
                          lambda k, g, vh=vh: wvb.ap(
                              0, 128, k * VD + vh * GV2 + g * 512,
                              min(512, GV2 - g * 512)),
                          KC, bks, GV2,
                          waits=[ee] if vh == 0 else [ke], ktok=twV)
                if vh == 1:
                    xpf_free[sb] = mV
                ve = None
                for g in range((GV2 + 511) // 512):
                    cn = min(512, GV2 - g * 512)
                    ve = vop(lambda v, g=g, cn=cn, vh=vh, vchk=vchk, bks=bks:
                             v.tensor_copy(
                        vchk.ap(0, 128, vh * GV2 + g * 512, cn),
                        bks[g][0:128, 0:cn]),
                        waits=[mV, vchk_freeb[sb]] + vchk.birth)
                for g in range((GV2 + 511) // 512):
                    bset(bks[g], ve)
                ve_tok[vh] = ve
            s1w = GH * M
            s1tot = NG * s1w
            s1last = {}
            for g in range(NG):
                flat = g * s1w
                b, co = flat // 512, flat % 512
                bk = FB[4 + b]
                lastb = (flat + s1w == min((b + 1) * 512, s1tot))
                for i in range(GH):
                    bh = g * GH + i
                    fi = (co == 0 and i == 0)
                    la = (lastb and i == GH - 1)
                    tok = mm(bk[0:64, co + i * M: co + (i + 1) * M],
                             kchk.ap(0, 128, bh * DK, DK),
                             echk.ap(0, 128, bh * M, M),
                             start=fi, stop=la,
                             waits=[ee, ke, bfree(bk)] if fi else (),
                             inc=la)
                    if la:
                        s1last[b] = tok
            a1toks = {}
            for b in sorted(s1last):
                cn = min(512, s1tot - b * 512)
                a1toks[b] = vop(lambda v, b=b, cn=cn: v.tensor_tensor(
                    s1f.ap(0, 64, b * 512, cn), s1f.ap(0, 64, b * 512, cn),
                    FB[4 + b][0:64, 0:cn], ALU.add), waits=[s1last[b]])
            for g in range(NG):
                s1f_rdy[g] = a1toks[(g * s1w) // 512]
            a1_prev = a1toks
            for vh in range(2):
                bks = [FB[0], FB[1]] if vh == 0 else [FB[2], FB[3]]
                ve = ve_tok[vh]
                s2last = {}
                h_lo = vh * (H // 2)
                for idx in range(H // 2):
                    bh = h_lo + idx
                    flat = idx * DV
                    b, co = flat // 512, flat % 512
                    bk = bks[b]
                    fi = (co == 0)
                    la = (flat + DV == min((b + 1) * 512, s2tot // 2))
                    tok = mm(bk[0:64, co:co + DV],
                             echk.ap(0, 128, bh * M, M),
                             vchk.ap(0, 128, bh * DV, DV),
                             start=fi, stop=la,
                             waits=[ee, ve, bfree(bk)] if fi else (),
                             inc=la)
                    if la:
                        s2last[b] = tok
                for b in sorted(s2last):
                    cn = min(512, s2tot // 2 - b * 512)
                    a2 = vop(lambda v, b=b, cn=cn, vh=vh, bks=bks:
                             v.tensor_tensor(
                        s2f.ap(0, 64, vh * (s2tot // 2) + b * 512, cn),
                        s2f.ap(0, 64, vh * (s2tot // 2) + b * 512, cn),
                        bks[b][0:64, 0:cn], ALU.add), waits=[s2last[b]])
                    bset(bks[b], a2)
                    for g in range(NG):
                        for i in range(GH):
                            fl = (g * GH + i) * DV - vh * (s2tot // 2)
                            if 0 <= fl < s2tot // 2 and fl // 512 == b:
                                s2f_rdy[g] = a2
            ps_pend = (echk, sb)
            kchk_freeb[sb] = P.cur("MM")
            vchk_freeb[sb] = P.cur("MM")
        ps_pass(*ps_pend)

        for g in range(NG):
            s1b_rdy[g] = vop(lambda v, g=g: v.tensor_copy(
                s1b.ap(0, 64, g * GH * M, GH * M),
                s1f.ap(0, 64, g * GH * M, GH * M)), waits=[s1f_rdy[g]])
            s2b_rdy[g] = vop(lambda v, g=g: v.tensor_copy(
                s2b.ap(0, 64, g * GH * DV, GH * DV),
                s2f.ap(0, 64, g * GH * DV, GH * DV)), waits=[s2f_rdy[g]])
            psb_rdy[g] = vop(lambda v, g=g: v.tensor_copy(
                psb.ap(0, 1, g * GH * M, GH * M),
                psf.ap(0, 1, g * GH * M, GH * M)), waits=[psf_rdy[g]])

        pb_end = phase_barrier()
        for b in (vchkb[0], vchkb[1], kchkb[0], kchkb[1], echkb[0], echkb[1],
                  xpf[1], xpf[0]):
            AR.freebuf(b, toks=pb_end)
        _spare = AR.alloc("spare", [128, 16 * 1024], BF16)
        AR.freebuf(_spare, toks=[])

        utb = v2tb = w2b = b1_sb = None
        tUT = tV2 = tW2 = tB1 = None

        pre_xmf = [None]
        ws_pre = [None]
        h1_pre = [None, None]
        for half in range(2):
            tok0 = half * TH

            ovhh = [None, None]

            def ov_ap(c, col, width):
                b = col // GV
                assert col % GV + width <= GV
                return ovhh[b].ap(0, 128, c * GV + col % GV, width)
            if half == 1 and pre_xmf[0] is not None:
                xmf = pre_xmf[0]
            else:
                xmf = AR.alloc("xmf", [128, NHC * KC * 128], BF16)
            eh = AR.alloc("eh", [128, NHC * MD], BF16)

            def xslc(c):
                def f(k, c=c):
                    return xmf.ap(0, 128, (c * KC + k) * 128, 128)
                return f

            if half == 0:
                wsb2, tws2 = wsb, twS
            elif ws_pre[0] is not None:
                wsb2, tws2 = ws_pre[0]
            else:
                wsb2, tws2 = load_w("wsb2", ws, D, MD)
            xe_tok = [None] * NHC
            ee_tok = [None] * NHC
            ke_tok = [None] * NHC
            ve_tok = [None] * NHC
            qe_tok = [None] * NHC
            for c in range(NHC):
                bk = [FB[0], FB[1]] if c % 2 == 0 else [FB[2], FB[3]]
                sb = c % 2
                st = strip_load(x_main, tok0 + c * 128, sb)
                xe = x_to_fm(sb, st, xmf, c * KC * 128)
                xe_tok[c] = xe
                mS = gemm(xslc(c), lambda k, g: wsb2.ap(0, 128, k * MD + g * 512,
                                                        min(512, MD - g * 512)),
                          KC, bk, MD, waits=[xe], ktok=tws2)
                cl = None
                for g in range((MD + 511) // 512):
                    cn = min(512, MD - g * 512)
                    cl = vop(lambda v, g=g, cn=cn, bk=bk: v.tensor_scalar(
                        bk[g][0:128, 0:cn], bk[g][0:128, 0:cn],
                        CLM, CLX, ALU.max, ALU.min), waits=[mS])
                ee = None
                for g in range((MD + 511) // 512):
                    cn = min(512, MD - g * 512)
                    ee = sop(lambda s, c=c, g=g, cn=cn, bk=bk: s.activation(
                        eh.ap(0, 128, c * MD + g * 512, cn),
                        bk[g][0:128, 0:cn], AF.Exp,
                        bias=0.0, scale=1.0), waits=[cl] + eh.birth)
                for g in range((MD + 511) // 512):
                    bset(bk[g], ee)
                ee_tok[c] = ee
            bS = phase_barrier()
            AR.freebuf(wsb2, toks=bS)
            wqb2, twq2 = load_w("wqb2", wq, D, QD)
            if half == 0:
                wkb2, twk2 = wkb, twK
                wv_pre = [None, None]
            else:
                wkb2, twk2 = h1_pre[0]
                wv_pre = [h1_pre[1], None]

            ktm = AR.alloc("ktm", [128, NHC * QD], BF16)
            for c in range(NHC):
                bk = [FB[0], FB[1]] if c % 2 == 0 else [FB[2], FB[3]]
                mK = gemm(xslc(c), lambda k, g: wkb2.ap(0, 128, k * QD + g * 512,
                                                        min(512, QD - g * 512)),
                          KC, bk, QD, waits=[xe_tok[c]], ktok=twk2)
                ke = None
                for g in range((QD + 511) // 512):
                    cn = min(512, QD - g * 512)
                    ke = sop(lambda s, c=c, g=g, cn=cn, bk=bk: s.copy(
                        ktm.ap(0, 128, c * QD + g * 512, cn), bk[g][0:128, 0:cn]),
                        waits=[mK] + ktm.birth)
                for g in range((QD + 511) // 512):
                    bset(bk[g], ke)
                ke_tok[c] = ke
            bK = phase_barrier()
            AR.freebuf(wkb2, toks=bK)
            del bK
            if half == 1:
                wv_pre[1] = load_w("wvb2", wv, D, VD, col0=GV, ncols=GV)

            wg_pre = [None, None]
            vtm = AR.alloc("vtm", [128, NHC * VD], BF16)
            for vh in range(2):
                if half == 0:
                    wvb2, twv2 = wvb, twV
                    wv_ap = (lambda k, g, vh=vh: wvb.ap(
                        0, 128, k * VD + vh * GV + g * 512,
                        min(512, GV - g * 512)))
                else:
                    wvb2, twv2 = wv_pre[vh]
                    wv_ap = (lambda k, g, w=wvb2: w.ap(
                        0, 128, k * GV + g * 512, min(512, GV - g * 512)))
                vco = 0
                for c in range(NHC):
                    bk = [FB[0], FB[1]] if c % 2 == 0 else [FB[2], FB[3]]
                    mV = gemm(xslc(c), wv_ap,
                              KC, bk, GV, waits=[xe_tok[c]], ktok=twv2)
                    ve = None
                    for g in range((GV + 511) // 512):
                        cn = min(512, GV - g * 512)
                        ve = sop(lambda s, c=c, g=g, cn=cn, vh=vh, bk=bk:
                                 s.copy(
                            vtm.ap(0, 128, c * VD + vh * GV + g * 512, cn),
                            bk[g][0:128, 0:cn]), waits=[mV] + vtm.birth)
                    for g in range((GV + 511) // 512):
                        bset(bk[g], ve)
                    ve_tok[c] = ve
                bV = phase_barrier()
                if half == 1 or vh == 1:
                    AR.freebuf(wvb2, toks=bV)

            qtm = AR.alloc("qtm", [128, NHC * QD], BF16)
            for c in range(NHC):
                bk = [FB[0], FB[1]] if c % 2 == 0 else [FB[2], FB[3]]
                mQ = gemm(xslc(c), lambda k, g: wqb2.ap(0, 128, k * QD + g * 512,
                                                        min(512, QD - g * 512)),
                          KC, bk, QD, waits=[xe_tok[c]], ktok=twq2)
                qe = None
                for g in range((QD + 511) // 512):
                    cn = min(512, QD - g * 512)
                    qe = sop(lambda s, c=c, g=g, cn=cn, bk=bk: s.activation(
                        qtm.ap(0, 128, c * QD + g * 512, cn), bk[g][0:128, 0:cn],
                        AF.Copy, bias=0.0, scale=QSC), waits=[mQ] + qtm.birth)
                for g in range((QD + 511) // 512):
                    bset(bk[g], qe)
                qe_tok[c] = qe
            bQ = phase_barrier()
            AR.freebuf(wqb2, toks=bQ)
            wg_pre[0] = load_w("wgb2", wg, D, VD, col0=0, ncols=GV)

            ovhh[0] = AR.alloc("ovh", [128, NHC * GV], F32)
            ovhh[1] = AR.alloc("ovh", [128, NHC * GV], F32)
            kqfmb = [AR.alloc("kqfm", [64, 2 * GH * 128], BF16) for _ in range(2)]
            efmb = [AR.alloc("efm", [64, GH * 128], BF16) for _ in range(3)]
            cfmb = [AR.alloc("cfm", [64, GH * 128], BF16) for _ in range(2)]
            abfb = [AR.alloc("abf", [128, GH * 128], BF16) for _ in range(2)]
            wbfb = [AR.alloc("wbf", [128, GH * 128], BF16) for _ in range(2)]
            rbuf = AR.alloc("rbuf", [128, GH * M], F32)
            okb = AR.alloc("okb", [128, GH * M], F32)
            ebuf = AR.alloc("ebuf", [128, GH * M], F32)
            mx4 = AR.alloc("mx4", [128, GH], F32)
            sm4 = AR.alloc("sm4", [128, GH], F32)
            cbfb = [AR.alloc("cbf", [128, GH * M], BF16) for _ in range(2)]
            arwb = [AR.alloc("arw", [128, GH * 128], BF16) for _ in range(2)]
            wrwb = [AR.alloc("wrw", [128, GH * 128], BF16) for _ in range(2)]
            NIT = NHC * NG
            tok_te = [None] * NIT
            tok_eA = [None] * NIT
            tok_mtN = [None] * NIT
            tok_cC = [None] * NIT
            tok_tc = [None] * NIT
            ex_prev = [None]

            def scp(out_ap, in_ap, waits=()):
                return sop(lambda s, o=out_ap, i=in_ap: s.copy(o, i), waits=waits)

            def p_front(it):
                c, g = divmod(it, NG)
                h0 = g * GH
                kqf = kqfmb[it % 2]
                efm = efmb[it % 3]
                srcs = [ktm.ap(0, 128, c * QD + (h0 + j) * DK, DK)
                        for j in range(GH)] + \
                       [qtm.ap(0, 128, c * QD + (h0 + j) * DK, DK)
                        for j in range(GH)]
                mt = None
                for j, sap in enumerate(srcs):
                    mt = mm(TBK[0][0:64, j * 128:(j + 1) * 128], sap,
                            idb_sb.h[0:128, 0:128], start=(j == 0),
                            stop=(j == len(srcs) - 1),
                            waits=([t_idb, bfree(TBK[0]), ke_tok[c],
                                    qe_tok[c]]) if j == 0 else (),
                            inc=(j == len(srcs) - 1), tp=True)
                ev = scp(kqf.ap(0, 64, 0, 2 * GH * 128),
                         TBK[0][0:64, 0:2 * GH * 128], waits=[mt] + kqf.birth)
                bset(TBK[0], ev)
                mt = None
                for j in range(GH):
                    mt = mm(TBK[1][0:64, j * 128:(j + 1) * 128],
                            eh.ap(0, 128, c * MD + (h0 + j) * M, M),
                            idb_sb.h[0:128, 0:128], start=(j == 0),
                            stop=(j == GH - 1),
                            waits=[t_idb, bfree(TBK[1]), ee_tok[c]]
                            if j == 0 else (),
                            inc=(j == GH - 1), tp=True)
                ev2 = scp(efm.ap(0, 64, 0, GH * 128), TBK[1][0:64, 0:GH * 128],
                          waits=[mt] + efm.birth)
                bset(TBK[1], ev2)
                tok_te[it] = ev2
                mtA = None
                for j in range(GH):
                    mtA = mm(FB[0][0:128, j * 128:(j + 1) * 128],
                             kqf.ap(0, 64, j * 128, 128),
                             kqf.ap(0, 64, GH * 128 + j * 128, 128),
                             start=(j == 0), stop=(j == GH - 1),
                             waits=[ev, bfree(FB[0])] if j == 0 else (),
                             inc=(j == GH - 1))
                abf = abfb[it % 2]
                arw = arwb[it % 2]
                cA = sop(lambda s, a=arw: s.copy(a.ap(0, 128, 0, GH * 128),
                                                 FB[0][0:128, 0:GH * 128]),
                         waits=[mtA] + arw.birth)
                bset(FB[0], cA)
                P.waits("gpsimd", [cA, t_lmb] + abf.birth)
                eA = P.inc("gpsimd", "GPC", _eager(
                    lambda g, a=abf, r=arw: g.tensor_tensor(
                        a.ap(0, 128, 0, GH * 128), r.ap(0, 128, 0, GH * 128),
                        lmb_sb.ap3(0, 128, 0, [[0, GH], [1, 128]]), ALU.mult)))
                tok_eA[it] = eA

            def p_np(it):
                c, g = divmod(it, NG)
                h0 = g * GH
                kqf = kqfmb[it % 2]
                abf = abfb[it % 2]
                NPB = FB[2] if it % 2 == 0 else FB[3]
                wN = [tok_eA[it], s1b_rdy[g], psb_rdy[g], t_lmb, t_or,
                      bfree(NPB), ee_tok[c]]
                first = True
                for j in range(GH):
                    bh = h0 + j
                    mm(NPB[0:128, j * M:(j + 1) * M],
                       kqf.ap(0, 64, GH * 128 + j * 128, 128),
                       s1b.ap(0, 64, bh * M, M), start=first, stop=False,
                       waits=wN if first else ())
                    first = False
                mm(NPB[0:128, GH * M:2 * GH * M], lmb_sb.h[0:128, 0:128],
                   eh.ap(0, 128, c * MD + h0 * M, GH * M), start=False,
                   stop=False)
                mm(NPB[0:128, GH * M:2 * GH * M], or_sb.h[0:1, 0:128],
                   psb.ap(0, 1, h0 * M, GH * M), start=False, stop=False)
                mtN = None
                for j in range(GH):
                    bh = h0 + j
                    last = (j == GH - 1)
                    mtN = mm(NPB[0:128, j * M:(j + 1) * M],
                             abf.ap(0, 128, j * 128, 128),
                             eh.ap(0, 128, c * MD + bh * M, M),
                             start=False, stop=last, inc=last)
                tok_mtN[it] = mtN

            def p_pv(it):
                NPB = FB[2] if it % 2 == 0 else FB[3]
                cbf = cbfb[it % 2]
                lnp = sop(lambda s: s.activation(
                    rbuf.ap(0, 128, 0, GH * M), NPB[0:128, GH * M:2 * GH * M],
                    AF.Ln, bias=0.0, scale=1.0),
                    waits=[tok_mtN[it]] + rbuf.birth)
                vop(lambda v: v.memset(rbuf.h[0:1, 0:1], 0)) if False else None
                sop(lambda s: s.activation(
                    rbuf.ap(0, 128, 0, GH * M), rbuf.ap(0, 128, 0, GH * M),
                    AF.Exp, bias=0.0, scale=-1.0), waits=[lnp])
                ok = vop(lambda v: v.tensor_tensor(
                    okb.ap(0, 128, 0, GH * M), NPB[0:128, 0:GH * M],
                    rbuf.ap(0, 128, 0, GH * M), ALU.mult),
                    waits=[ex_prev[0], P.cur("SC")] + okb.birth)
                bset(NPB, ok)
                ex = sop(lambda s: s.activation(
                    ebuf.ap(0, 128, 0, GH * M), okb.ap(0, 128, 0, GH * M),
                    AF.Exp, bias=0.0, scale=1.0),
                    waits=[P.cur("VE")] + ebuf.birth)
                ex_prev[0] = ex
                vop(lambda v: v.tensor_reduce(
                    sm4.ap(0, 128, 0, GH),
                    ebuf.ap3(0, 128, 0, [[M, GH], [1, M]]), AX.X, ALU.add),
                    waits=[ex] + sm4.birth)
                vop(lambda v: v.reciprocal(sm4.ap(0, 128, 0, GH),
                                           sm4.ap(0, 128, 0, GH)))
                vop(lambda v: v.tensor_tensor(
                    ebuf.ap(0, 128, 0, GH * M), ebuf.ap(0, 128, 0, GH * M),
                    rbuf.ap(0, 128, 0, GH * M), ALU.mult))
                tok_cC[it] = vop(lambda v, cb=cbf: v.tensor_tensor(
                    cb.ap(0, 128, 0, GH * M), ebuf.ap(0, 128, 0, GH * M),
                    sm4.ap3(0, 128, 0, [[1, GH], [0, M]]), ALU.mult),
                    waits=([tok_tc[it - 2]] if it >= 2 else []) + cbf.birth)

            def p_back(it):
                c, g = divmod(it, NG)
                h0 = g * GH
                efm = efmb[it % 3]
                cbf = cbfb[it % 2]
                cfm = cfmb[it % 2]
                wbf = wbfb[it % 2]
                mt = None
                for j in range(GH):
                    mt = mm(TBK[1][0:64, j * 128:(j + 1) * 128],
                            cbf.ap(0, 128, j * M, M), idb_sb.h[0:128, 0:128],
                            start=(j == 0), stop=(j == GH - 1),
                            waits=([tok_cC[it], t_idb, bfree(TBK[1])]
                                   if j == 0 else ()),
                            inc=(j == GH - 1), tp=True)
                tc = scp(cfm.ap(0, 64, 0, GH * 128), TBK[1][0:64, 0:GH * 128],
                         waits=[mt] + cfm.birth)
                bset(TBK[1], tc)
                tok_tc[it] = tc
                mtW = None
                for j in range(GH):
                    mtW = mm(FB[4][0:128, j * 128:(j + 1) * 128],
                             efm.ap(0, 64, j * 128, 128),
                             cfm.ap(0, 64, j * 128, 128),
                             start=(j == 0), stop=(j == GH - 1),
                             waits=([tc, tok_te[it], bfree(FB[4])]
                                    if j == 0 else ()),
                             inc=(j == GH - 1))
                wrw = wrwb[it % 2]
                cW = sop(lambda s, w=wrw: s.copy(w.ap(0, 128, 0, GH * 128),
                                                 FB[4][0:128, 0:GH * 128]),
                         waits=[mtW] + wrw.birth)
                bset(FB[4], cW)
                P.waits("gpsimd", [cW, t_lmb] + wbf.birth)
                eW = P.inc("gpsimd", "GPC", _eager(
                    lambda g, w=wbf, r=wrw: g.tensor_tensor(
                        w.ap(0, 128, 0, GH * 128), r.ap(0, 128, 0, GH * 128),
                        lmb_sb.ap3(0, 128, 0, [[0, GH], [1, 128]]), ALU.mult)))
                first = True
                for j in range(GH):
                    bh = h0 + j
                    mm(FB[5][0:128, j * 128:(j + 1) * 128],
                       cfm.ap(0, 64, j * 128, 128), s2b.ap(0, 64, bh * DV, DV),
                       start=first, stop=False,
                       waits=[tc, s2b_rdy[g], bfree(FB[5]), ve_tok[c]]
                       if first else ())
                    first = False
                mtO = None
                for j in range(GH):
                    bh = h0 + j
                    last = (j == GH - 1)
                    mtO = mm(FB[5][0:128, j * 128:(j + 1) * 128],
                             wbf.ap(0, 128, j * 128, 128),
                             vtm.ap(0, 128, c * VD + bh * DV, DV),
                             start=False, stop=last,
                             waits=[eW] if j == 0 else (), inc=last)
                eO = None
                col0 = g * GH * DV
                off = 0
                while off < GH * DV:
                    w = min(GH * DV - off, GV - (col0 + off) % GV)
                    eO = scp(ov_ap(c, col0 + off, w),
                             FB[5][0:128, off:off + w],
                             waits=[mtO] + ovhh[0].birth + ovhh[1].birth)
                    off += w
                bset(FB[5], eO)
                mt2 = None
                for j in range(GH):
                    bh = h0 + j
                    last = (j == GH - 1)
                    mt2 = mm(FB[4][0:64, j * DV:(j + 1) * DV],
                             eh.ap(0, 128, c * MD + bh * M, M),
                             vtm.ap(0, 128, c * VD + bh * DV, DV),
                             start=(j == 0), stop=last,
                             waits=[eW] if j == 0 else (), inc=last)
                wS1 = [bfree(FB[1]), t_oc, ke_tok[c]]
                for j in range(GH - 1):
                    bh = h0 + j
                    mm(FB[1][0:64, j * M:(j + 1) * M],
                       ktm.ap(0, 128, c * QD + bh * DK, DK),
                       eh.ap(0, 128, c * MD + bh * M, M),
                       start=(j == 0), stop=False, waits=wS1 if j == 0 else ())
                mm(FB[1][0:1, GH * M:2 * GH * M], oc_sb.h[0:128, 0:1],
                   eh.ap(0, 128, c * MD + h0 * M, GH * M),
                   start=False, stop=False, waits=wS1 if GH == 1 else ())
                bh = h0 + GH - 1
                mt1 = mm(FB[1][0:64, (GH - 1) * M:GH * M],
                         ktm.ap(0, 128, c * QD + bh * DK, DK),
                         eh.ap(0, 128, c * MD + bh * M, M),
                         start=False, stop=True, inc=True)
                a1 = vop(lambda v, g=g: v.tensor_tensor(
                    s1f.ap(0, 64, g * GH * M, GH * M),
                    s1f.ap(0, 64, g * GH * M, GH * M),
                    FB[1][0:64, 0:GH * M], ALU.add), waits=[mt1, s1f_rdy[g]])
                a2 = vop(lambda v, g=g: v.tensor_tensor(
                    psf.ap(0, 1, g * GH * M, GH * M),
                    psf.ap(0, 1, g * GH * M, GH * M),
                    FB[1][0:1, GH * M:2 * GH * M], ALU.add),
                    waits=[psf_rdy[g]])
                s1f_rdy[g] = a1
                psf_rdy[g] = a2
                bset(FB[1], a2)
                a3 = vop(lambda v, g=g: v.tensor_tensor(
                    s2f.ap(0, 64, g * GH * DV, GH * DV),
                    s2f.ap(0, 64, g * GH * DV, GH * DV),
                    FB[4][0:64, 0:GH * DV], ALU.add), waits=[mt2, s2f_rdy[g]])
                s2f_rdy[g] = a3
                bset(FB[4], a3)
                if g == NG - 1:
                    w1s = [s1f_rdy[gg] for gg in range(NG)] + [tok_mtN[it]]
                    t1 = scp(s1b.ap(0, 64, 0, H * M), s1f.ap(0, 64, 0, H * M),
                             waits=w1s)
                    w2s = [s2f_rdy[gg] for gg in range(NG)] + [mtO]
                    t2 = scp(s2b.ap(0, 64, 0, H * DV), s2f.ap(0, 64, 0, H * DV),
                             waits=w2s)
                    w3s = [psf_rdy[gg] for gg in range(NG)] + [tok_mtN[it]]
                    t3 = scp(psb.ap(0, 1, 0, H * M), psf.ap(0, 1, 0, H * M),
                             waits=w3s)
                    for gg in range(NG):
                        s1b_rdy[gg] = t1
                        s2b_rdy[gg] = t2
                        psb_rdy[gg] = t3

            for it in range(NIT + 1):
                if it < NIT:
                    p_front(it)
                if 0 <= it - 1 < NIT:
                    p_back(it - 1)
                if it < NIT:
                    p_np(it)
                    p_pv(it)
            bPAT = phase_barrier()
            for b in (cbfb + arwb + wrwb + [sm4, mx4, ebuf, okb, rbuf]
                      + wbfb + abfb + cfmb + efmb + kqfmb
                      + [qtm, ktm, eh, vtm]):
                AR.freebuf(b, toks=bPAT)

            ofm = AR.alloc("ofm", [128, VDC * TH], BF16)
            gsc2 = [AR.alloc("gsc", [128, GV], F32) for _ in range(2)]
            gsc2_free = [None, None]
            sgs2 = [AR.alloc("sgs", [128, GV], F32) for _ in range(2)]
            obs2 = [AR.alloc("obs", [128, GV], F32) for _ in range(2)]
            otb2 = [AR.alloc("otb", [128, GV], BF16) for _ in range(2)]
            otb_free2 = [None, None]
            sgs_free2 = [None, None]
            obs_free2 = [None, None]
            msq = AR.alloc("msq", [128, NHC * HH], F32)
            ofm_tok = [None] * NHC
            wg_pre[1] = load_w("wgb2", wg, D, VD, col0=GV, ncols=GV)
            wo_pre = [None, None]
            for vh in range(2):
                if vh == 1:
                    wo_pre[0] = load_w("wob2", wo, VD, D, col0=0, ncols=D2)
                wgb2, twg2 = wg_pre[vh]
                sq = None
                for c in range(NHC):
                    obs = obs2[c % 2]
                    sq = sop(lambda s, c=c, vh=vh, obs=obs: s.activation(
                        obs.ap(0, 128, 0, GV),
                        ov_ap(c, vh * GV, GV), AF.Square,
                        bias=0.0, scale=1.0),
                        waits=bPAT + [obs_free2[c % 2]] + obs.birth)
                    obs_free2[c % 2] = vop(
                        lambda v, c=c, obs=obs: v.tensor_reduce(
                            msq.ap(0, 128, c * HH, HH),
                            obs.ap3(0, 128, 0, [[DV, HH], [1, DV]]),
                            AX.X, ALU.add),
                        waits=[sq] + msq.birth)
                si = sop(lambda s: s.activation(msq.ap(0, 128, 0, NHC * HH),
                                                msq.ap(0, 128, 0, NHC * HH),
                                                AF.Sqrt, bias=EPS, scale=1.0 / DV),
                         waits=[P.cur("VE")])
                iv = vop(lambda v: v.reciprocal(msq.ap(0, 128, 0, NHC * HH),
                                                msq.ap(0, 128, 0, NHC * HH)),
                         waits=[si])
                mG_tok = [None] * NHC
                gv_tok = [None] * NHC

                mG_of = [None] * NHC

                def g_proj(c):
                    bk = [FB[0], FB[1]] if c % 2 == 0 else [FB[2], FB[3]]
                    toks = []
                    for b in range((GV + 511) // 512):
                        cn = min(512, GV - b * 512)
                        toks.append(gemm(
                            xslc(c),
                            lambda k, g, b=b, cn=cn: wgb2.ap(
                                0, 128, k * GV + b * 512, cn),
                            KC, [bk[b]], cn,
                            waits=[xe_tok[c]] if b == 0 else (), ktok=twg2))
                    mG_of[c] = toks

                def g_evac(c):
                    bk = [FB[0], FB[1]] if c % 2 == 0 else [FB[2], FB[3]]
                    gvs = []
                    for g in range((GV + 511) // 512):
                        cn = min(512, GV - g * 512)
                        gvs.append(vop(
                            lambda v, g=g, cn=cn, bk=bk, c=c: v.tensor_copy(
                                gsc2[c % 2].ap(0, 128, g * 512, cn),
                                bk[g][0:128, 0:cn]),
                            waits=[mG_of[c][g], gsc2_free[c % 2]]
                            + gsc2[c % 2].birth))
                    for g in range((GV + 511) // 512):
                        bset(bk[g], gvs[-1])
                    gv_tok[c] = gvs

                def g_gate(c, vh):
                    gsc = gsc2[c % 2]
                    sgs = sgs2[c % 2]
                    obs = obs2[c % 2]
                    otb = otb2[c % 2]
                    sg = None
                    for g in range((GV + 511) // 512):
                        cn = min(512, GV - g * 512)
                        sg = sop(lambda s, g=g, cn=cn, sgs=sgs, gsc=gsc:
                                 s.activation(
                            sgs.ap(0, 128, g * 512, cn),
                            gsc.ap(0, 128, g * 512, cn),
                            AF.Silu, bias=0.0, scale=1.0),
                            waits=[gv_tok[c][g], sgs_free2[c % 2]] + sgs.birth)
                    gsc2_free[c % 2] = sg
                    gop(lambda g, c=c, vh=vh, obs=obs: g.tensor_tensor(
                        obs.ap(0, 128, 0, GV),
                        ov_ap(c, vh * GV, GV),
                        msq.ap3(0, 128, c * HH, [[1, HH], [0, DV]]),
                        ALU.mult), waits=[iv, obs_free2[c % 2]] + obs.birth)
                    o2 = gop(lambda g, obs=obs: g.tensor_tensor(
                        obs.ap(0, 128, 0, GV), obs.ap(0, 128, 0, GV),
                        gnw_sb.ap3(0, 128, 0, [[0, HH], [1, DV]]),
                        ALU.mult), waits=[t_gnw])
                    o3 = vop(lambda v, obs=obs, sgs=sgs, otb=otb:
                             v.tensor_tensor(
                        otb.ap(0, 128, 0, GV), obs.ap(0, 128, 0, GV),
                        sgs.ap(0, 128, 0, GV), ALU.mult),
                        waits=[sg, o2, otb_free2[c % 2]] + otb.birth)
                    sgs_free2[c % 2] = o3
                    obs_free2[c % 2] = o3
                    for j0 in range(0, HH, 4):
                        nj = min(4, HH - j0)
                        bnk = TBK[j0 // 4 % 2]
                        mt = None
                        for j in range(nj):
                            mt = mm(bnk[0:128, j * 128:(j + 1) * 128],
                                    otb.ap(0, 128, (j0 + j) * DV, DV),
                                    idb_sb.h[0:128, 0:128],
                                    start=(j == 0), stop=(j == nj - 1),
                                    waits=[o3, t_idb, bfree(bnk)] if j == 0 else (),
                                    inc=(j == nj - 1), tp=True)
                        evt = vop(lambda v, b=bnk, nj=nj, c=c, j0=j0, vh=vh:
                                  v.tensor_copy(
                                      ofm.ap3(0, 128,
                                              (vh * HH + j0) * TH + c * 128,
                                              [[TH, nj], [1, 128]]),
                                      b[0:128, 0:nj * 128]),
                                  waits=[mt] + ofm.birth)
                        bset(bnk, evt)
                        ofm_tok[c] = evt
                    return P.cur("MM")

                for c in range(NHC + 1):
                    if c < NHC:
                        g_proj(c)
                    if c - 1 >= 0:
                        otb_free2[(c - 1) % 2] = g_gate(c - 1, vh)
                    if c < NHC:
                        g_evac(c)
                bPAT = phase_barrier()
                AR.freebuf(wgb2, toks=bPAT)
                AR.freebuf(ovhh[vh], toks=bPAT)

            for b in (msq, otb2[0], otb2[1], obs2[0], obs2[1], sgs2[0],
                      sgs2[1], gsc2[0], gsc2[1]):
                AR.freebuf(b, toks=bPAT)
            AR.freebuf(xmf, toks=bPAT)

            x1bufs = AR.alloc("x1bufs", [128, NHC * D], F32)
            xa_tok = [None] * NHC
            stt = AR.alloc("stt", [128, 4 * NHC], F32)
            scrf = AR.alloc("scrf", [128, D], F32)
            scrb = AR.alloc("scrb", [128, D], BF16)
            scrf_free = scrb_free = None
            sh_free = [None, None]
            wo_pre[1] = load_w("wob2", wo, VD, D, col0=D2, ncols=D2)
            for dh in range(2):
                wob2, two2 = wo_pre[dh]
                for c in range(NHC):
                    bk = [FB[0], FB[1]] if c % 2 == 0 else [FB[2], FB[3]]
                    mW = gemm(lambda j, c=c: ofm.ap(0, 128, j * TH + c * 128, 128),
                              lambda j, g: wob2.ap(0, 128, j * D2 + g * 512,
                                                   min(512, D2 - g * 512)),
                              VDC, bk, D2, waits=[ofm_tok[c]], ktok=two2)
                    sb = c % 2
                    rst = strip_load(x_main, tok0 + c * 128, sb)
                    xa = None
                    for g in range((D2 + 511) // 512):
                        cn = min(512, D2 - g * 512)
                        xa = vop(lambda v, c=c, g=g, cn=cn, dh=dh, sb=sb, bk=bk:
                                 v.tensor_tensor(
                                     x1bufs.ap(0, 128, c * D + dh * D2 + g * 512, cn),
                                     bk[g][0:128, 0:cn],
                                     strips[sb].ap(0, 128, dh * D2 + g * 512, cn),
                                     ALU.add), waits=[mW, rst] + x1bufs.birth)
                    for g in range((D2 + 511) // 512):
                        bset(bk[g], xa)
                    strip_free[sb] = xa
                    ph = c % 2
                    if dh == 0:
                        vop(lambda v, c=c: v.tensor_reduce(
                            stt.ap(0, 128, c, 1), x1bufs.ap(0, 128, c * D, D2),
                            AX.X, ALU.add), waits=[xa] + stt.birth)
                        sq = sop(lambda s, c=c, ph=ph: s.activation(
                            scrf.ap(0, 128, ph * D2, D2),
                            x1bufs.ap(0, 128, c * D, D2),
                            AF.Square, bias=0.0, scale=1.0),
                            waits=[xa, sh_free[ph]] + scrf.birth)
                        sh_free[ph] = vop(lambda v, c=c, ph=ph: v.tensor_reduce(
                            stt.ap(0, 128, NHC + c, 1),
                            scrf.ap(0, 128, ph * D2, D2),
                            AX.X, ALU.add), waits=[sq])
                    else:
                        xa_tok[c] = xa
                        vop(lambda v, c=c: v.tensor_reduce(
                            stt.ap(0, 128, 2 * NHC + c, 1),
                            x1bufs.ap(0, 128, c * D + D2, D2),
                            AX.X, ALU.add), waits=[xa])
                        vop(lambda v, c=c: v.tensor_tensor(
                            stt.ap(0, 128, c, 1), stt.ap(0, 128, c, 1),
                            stt.ap(0, 128, 2 * NHC + c, 1), ALU.add))
                        sq = sop(lambda s, c=c, ph=ph: s.activation(
                            scrf.ap(0, 128, ph * D2, D2),
                            x1bufs.ap(0, 128, c * D + D2, D2),
                            AF.Square, bias=0.0, scale=1.0),
                            waits=[xa, sh_free[ph]])
                        r1b = vop(lambda v, c=c, ph=ph: v.tensor_reduce(
                            stt.ap(0, 128, 2 * NHC + c, 1),
                            scrf.ap(0, 128, ph * D2, D2),
                            AX.X, ALU.add), waits=[sq])
                        sh_free[ph] = r1b
                        vop(lambda v, c=c: v.tensor_tensor(
                            stt.ap(0, 128, NHC + c, 1),
                            stt.ap(0, 128, NHC + c, 1),
                            stt.ap(0, 128, 2 * NHC + c, 1), ALU.add))
                bPAT = phase_barrier()
                AR.freebuf(wob2, toks=bPAT)
            AR.freebuf(ofm, toks=bPAT)
            w1buf = AR.alloc("w1b", [128, KC * D2], BF16)
            tw12_0 = load_w_into(w1buf, w1, D, D, col0=0, ncols=D2,
                                 waits=list(w1buf.birth))
            if half == 0:
                pre_xmf[0] = AR.alloc("xmf", [128, NHC * KC * 128], BF16)
            else:
                v2tb, tV2 = load_w("v2tb", v2t, D, KR, side="right")
                b1_sb = AR.alloc("b1sb", [128, D], F32, side="right")
                tB1 = cdma(b1_sb[:, :], b1b, waits=b1_sb.birth,
                           sem=next_wsem())
                w2b, tW2 = load_w("w2b", w2, D, KE, side="right")
                utb, tUT = load_w("utb", ut, KR, D, side="right")
            scrf_free = sh_free[0]
            vop(lambda v: v.tensor_scalar(stt.ap(0, 128, 0, 2 * NHC),
                                          stt.ap(0, 128, 0, 2 * NHC),
                                          1.0 / D, None, ALU.mult))
            vop(lambda v: v.tensor_tensor(stt.ap(0, 128, 2 * NHC, NHC),
                                          stt.ap(0, 128, 0, NHC),
                                          stt.ap(0, 128, 0, NHC), ALU.mult))
            vop(lambda v: v.tensor_tensor(stt.ap(0, 128, NHC, NHC),
                                          stt.ap(0, 128, NHC, NHC),
                                          stt.ap(0, 128, 2 * NHC, NHC),
                                          ALU.subtract))
            s_i = sop(lambda s: s.activation(stt.ap(0, 128, NHC, NHC),
                                             stt.ap(0, 128, NHC, NHC),
                                             AF.Sqrt, bias=EPS, scale=1.0),
                      waits=[P.cur("VE")])
            vop(lambda v: v.reciprocal(stt.ap(0, 128, NHC, NHC),
                                       stt.ap(0, 128, NHC, NHC)), waits=[s_i])
            vop(lambda v: v.tensor_tensor(stt.ap(0, 128, 3 * NHC, NHC),
                                          stt.ap(0, 128, 0, NHC),
                                          stt.ap(0, 128, NHC, NHC), ALU.mult))
            nb = vop(lambda v: v.tensor_scalar(stt.ap(0, 128, 3 * NHC, NHC),
                                               stt.ap(0, 128, 3 * NHC, NHC),
                                               -1.0, None, ALU.mult))
            x2fm = AR.alloc("x2fm", [128, KC * TH], BF16)
            x2fm_tok = [None] * NHC
            for c in range(NHC):
                xn = sop(lambda s, c=c: s.activation(
                    x1bufs.ap(0, 128, c * D, D), x1bufs.ap(0, 128, c * D, D),
                    AF.Identity, bias=stt.h[0:128, 3 * NHC + c:3 * NHC + c + 1],
                    scale=stt.h[0:128, NHC + c:NHC + c + 1]), waits=[nb])
                xb = vop(lambda v, c=c: v.tensor_copy(
                    scrb.ap(0, 128, 0, D), x1bufs.ap(0, 128, c * D, D)),
                    waits=[xn, scrb_free] + scrb.birth)
                for k0 in range(0, KC, 4):
                    nk = min(4, KC - k0)
                    bnk = TBK[k0 // 4 % 2]
                    mt = None
                    for j in range(nk):
                        mt = mm(bnk[0:128, j * 128:(j + 1) * 128],
                                scrb.ap(0, 128, (k0 + j) * 128, 128),
                                idb_sb.h[0:128, 0:128],
                                start=(j == 0), stop=(j == nk - 1),
                                waits=[xb, t_idb, bfree(bnk)] if j == 0 else (),
                                inc=(j == nk - 1), tp=True)
                    evt = vop(lambda v, b=bnk, nk=nk, c=c, k0=k0: v.tensor_copy(
                        x2fm.ap3(0, 128, k0 * TH + c * 128, [[TH, nk], [1, 128]]),
                        b[0:128, 0:nk * 128]), waits=[mt] + x2fm.birth)
                    bset(bnk, evt)
                    scrb_free = P.cur("MM")
                    x2fm_tok[c] = evt
            bWo = phase_barrier()

            if half == 0:
                v2tb, tV2 = load_w("v2tb", v2t, D, KR, side="right")
                b1_sb = AR.alloc("b1sb", [128, D], F32, side="right")
                tB1 = cdma(b1_sb[:, :], b1b, waits=b1_sb.birth,
                           sem=next_wsem())
                w2b, tW2 = load_w("w2b", w2, D, KE, side="right")
                utb, tUT = load_w("utb", ut, KR, D, side="right")
            hfm = AR.alloc("hfm", [128, KC * 128], BF16)
            vxsb = AR.alloc("vxsb", [128, NHC * KR], BF16)
            alscb = [AR.alloc("alsc", [128, 2 * KE], F32) for _ in range(2)]
            vxwb = AR.alloc("vxwb", [128, KR], BF16)
            vxwf = AR.alloc("vxwf", [128, KRC * 128], BF16)
            st2 = AR.alloc("st2", [128, 4 * NHC], F32)

            mal = None
            alsc_free = [None, None]
            vxwb_free = None
            ts_pend = [None]

            def ts_mal(c, wh, hr):
                nonlocal mal, scrb_free
                HC2 = D2 // 128
                tts = []
                for k0 in range(0, HC2, 4):
                    nk = min(4, HC2 - k0)
                    bnk = TBK[k0 // 4 % 2]
                    mt = None
                    for j in range(nk):
                        mt = mm(bnk[0:128, j * 128:(j + 1) * 128],
                                scrb.ap(0, 128, (k0 + j) * 128, 128),
                                idb_sb.h[0:128, 0:128],
                                start=(j == 0), stop=(j == nk - 1),
                                waits=[hr, t_idb, bfree(bnk)] if j == 0 else (),
                                inc=(j == nk - 1), tp=True)
                    evt = vop(lambda v, b=bnk, nk=nk, k0=k0: v.tensor_copy(
                        hfm.ap(0, 128, k0 * 128, nk * 128),
                        b[0:128, 0:nk * 128]), waits=[mt] + hfm.birth)
                    bset(bnk, evt)
                    tts.append(evt)
                    scrb_free = P.cur("MM")
                for k2 in range(HC2):
                    first = (wh == 0 and c == 0 and k2 == 0)
                    lastg = (wh == 1 and c == NHC - 1 and k2 == HC2 - 1)
                    mal = mm(FB[5][0:128, c * KE:(c + 1) * KE],
                             hfm.ap(0, 128, k2 * 128, 128),
                             w2b.ap(0, 128, (wh * HC2 + k2) * KE, KE),
                             start=first, stop=lastg,
                             waits=([tW2(-1), bfree(FB[5])] + tts) if first
                             else tts,
                             inc=lastg)

            for wh in range(2):
                tw12 = (tw12_0 if wh == 0 else
                        load_w_into(w1buf, w1, D, D, col0=D2, ncols=D2,
                                    waits=[P.cur("MM")]))
                w1b2 = w1buf
                HC2 = D2 // 128
                for c in range(NHC):
                    bk = [FB[0], FB[1]] if c % 2 == 0 else [FB[2], FB[3]]
                    x2sl = (lambda k, c=c: x2fm.ap(0, 128, k * TH + c * 128, 128))
                    mH = gemm(x2sl,
                              lambda k, g: w1b2.ap(0, 128, k * D2 + g * 512,
                                                   min(512, D2 - g * 512)),
                              KC, bk, D2, waits=[x2fm_tok[c]], ktok=tw12)
                    if wh == 0:
                        mvx = None
                        for k in range(KC):
                            mvx = mm(FB[4][0:128, 0:min(512, KR)],
                                     x2fm.ap(0, 128, k * TH + c * 128, 128),
                                     v2tb.ap(0, 128, k * KR, min(512, KR)),
                                     start=(k == 0), stop=(k == KC - 1),
                                     waits=[tV2(k), bfree(FB[4]), x2fm_tok[c]]
                                     if k == 0 else [tV2(k)],
                                     inc=(k == KC - 1))
                        vxe = vop(lambda v, c=c: v.tensor_copy(
                            vxsb.ap(0, 128, c * KR, KR),
                            FB[4][0:128, 0:KR]), waits=[mvx] + vxsb.birth)
                        bset(FB[4], vxe)
                    if ts_pend[0] is not None:
                        ts_pend[0]()
                        ts_pend[0] = None
                    ha = None
                    for g in range((D2 + 511) // 512):
                        cn = min(512, D2 - g * 512)
                        ha = vop(lambda v, g=g, cn=cn, wh=wh, bk=bk: v.tensor_tensor(
                            scrf.ap(0, 128, g * 512, cn), bk[g][0:128, 0:cn],
                            b1_sb.h[0:128, wh * D2 + g * 512:wh * D2 + g * 512 + cn],
                            ALU.add), waits=[mH, tB1, scrf_free])
                    for g in range((D2 + 511) // 512):
                        bset(bk[g], ha)
                    hr = sop(lambda s: s.activation(
                        scrb.ap(0, 128, 0, D2), scrf.ap(0, 128, 0, D2), AF.Relu,
                        bias=0.0, scale=1.0), waits=[ha, scrb_free])
                    scrf_free = hr
                    ts_pend[0] = (lambda c=c, wh=wh, hr=hr: ts_mal(c, wh, hr))
            ts_pend[0]()
            ts_pend[0] = None
            bWo = phase_barrier()
            AR.freebuf(w1buf, toks=phase_barrier())
            if half == 0:
                ws_pre[0] = load_w("wsb2", ws, D, MD)
            mo_tok = [None] * NHC

            def ffn_x3_adds(c):
                x3 = None
                for g in range((D + 511) // 512):
                    cn = min(512, D - g * 512)
                    x3 = vop(lambda v, c=c, g=g, cn=cn: v.tensor_tensor(
                        x1bufs.ap(0, 128, c * D + g * 512, cn),
                        x1bufs.ap(0, 128, c * D + g * 512, cn),
                        FB[g][0:128, 0:cn], ALU.add), waits=[mo_tok[c]])
                for g in range((D + 511) // 512):
                    bset(FB[g], x3)
                return x3

            def ffn_x3_stats(c, x3):
                vop(lambda v, c=c: v.tensor_reduce(
                    st2.ap(0, 128, c, 1), x1bufs.ap(0, 128, c * D, D), AX.X,
                    ALU.add), waits=[x3] + st2.birth)
                sq = sop(lambda s, c=c: s.activation(
                    scrf.ap(0, 128, 0, D), x1bufs.ap(0, 128, c * D, D),
                    AF.Square, bias=0.0, scale=1.0), waits=[x3, scrf_free])
                return vop(lambda v, c=c: v.tensor_reduce(
                    st2.ap(0, 128, NHC + c, 1), scrf.ap(0, 128, 0, D), AX.X,
                    ALU.add), waits=[sq])

            def ffn_alpha(c):
                alsc = alscb[c % 2]
                al = vop(lambda v, c=c, alsc=alsc: v.tensor_tensor(
                    alsc.ap(0, 128, 0, KE), FB[5][0:128, c * KE:(c + 1) * KE],
                    b2_sb.h[0:128, 0:KE], ALU.add),
                    waits=[mal, t_b2, alsc_free[c % 2]] + alsc.birth)
                vop(lambda v, alsc=alsc: v.tensor_reduce(
                    alsc.ap(0, 128, KE, 1), alsc.ap(0, 128, 0, KE), AX.X,
                    ALU.max, negate=True))
                vop(lambda v, alsc=alsc: v.tensor_tensor(
                    alsc.ap(0, 128, 0, KE), alsc.ap(0, 128, 0, KE),
                    alsc.ap3(0, 128, KE, [[0, KE]]), ALU.add))
                ae = sop(lambda s, alsc=alsc: s.activation(
                    alsc.ap(0, 128, KE, KE), alsc.ap(0, 128, 0, KE), AF.Exp,
                    bias=0.0, scale=1.0), waits=[P.cur("VE")])
                vop(lambda v, alsc=alsc: v.tensor_reduce(
                    alsc.ap(0, 128, 0, 1), alsc.ap(0, 128, KE, KE), AX.X,
                    ALU.add), waits=[ae])
                vop(lambda v, alsc=alsc: v.reciprocal(alsc.ap(0, 128, 0, 1),
                                                      alsc.ap(0, 128, 0, 1)))
                av = vop(lambda v, alsc=alsc: v.tensor_tensor(
                    alsc.ap(0, 128, KE, KE), alsc.ap(0, 128, KE, KE),
                    alsc.ap3(0, 128, 0, [[0, KE]]), ALU.mult))
                alsc_free[c % 2] = ae
                return av

            av_tok = [None] * NHC
            av_tok[0] = ffn_alpha(0)
            x3a = None
            for c in range(NHC):
                alsc = alscb[c % 2]
                vw = vop(lambda v, c=c, alsc=alsc: v.tensor_tensor(
                    vxwb.ap(0, 128, 0, KR), vxsb.ap(0, 128, c * KR, KR),
                    alsc.ap3(0, 128, KE, [[1, KE], [0, R]]), ALU.mult),
                    waits=[av_tok[c], vxwb_free] + vxwb.birth)
                if c + 1 < NHC:
                    av_tok[c + 1] = ffn_alpha(c + 1)
                if c >= 1:
                    x3a = ffn_x3_adds(c - 1)
                for k0 in range(0, KRC, 4):
                    nk = min(4, KRC - k0)
                    bnk = TBK[k0 // 4 % 2]
                    mt = None
                    for j in range(nk):
                        mt = mm(bnk[0:128, j * 128:(j + 1) * 128],
                                vxwb.ap(0, 128, (k0 + j) * 128, 128),
                                idb_sb.h[0:128, 0:128],
                                start=(j == 0), stop=(j == nk - 1),
                                waits=[vw, t_idb, bfree(bnk)] if j == 0 else (),
                                inc=(j == nk - 1), tp=True)
                    evt = sop(lambda s, b=bnk, nk=nk, k0=k0: s.copy(
                        vxwf.ap(0, 128, k0 * 128, nk * 128), b[0:128, 0:nk * 128]),
                        waits=[mt] + vxwf.birth)
                    bset(bnk, evt)
                    vxwb_free = P.cur("MM")
                tvw = P.cur("SC")
                mo = gemm(lambda kr: vxwf.ap(0, 128, kr * 128, 128),
                          lambda kr, g: utb.ap(0, 128, kr * D + g * 512,
                                               min(512, D - g * 512)),
                          KRC, [FB[0], FB[1], FB[2], FB[3]], D,
                          waits=[tvw], ktok=tUT)
                mo_tok[c] = mo
                if c >= 1:
                    scrf_free = ffn_x3_stats(c - 1, x3a)
            scrf_free = ffn_x3_stats(NHC - 1, ffn_x3_adds(NHC - 1))
            vop(lambda v: v.tensor_scalar(st2.ap(0, 128, 0, 2 * NHC),
                                          st2.ap(0, 128, 0, 2 * NHC),
                                          1.0 / D, None, ALU.mult),
                waits=[scrf_free])
            vop(lambda v: v.tensor_tensor(st2.ap(0, 128, 2 * NHC, NHC),
                                          st2.ap(0, 128, 0, NHC),
                                          st2.ap(0, 128, 0, NHC), ALU.mult))
            vop(lambda v: v.tensor_tensor(st2.ap(0, 128, NHC, NHC),
                                          st2.ap(0, 128, NHC, NHC),
                                          st2.ap(0, 128, 2 * NHC, NHC),
                                          ALU.subtract))
            s_j = sop(lambda s: s.activation(st2.ap(0, 128, NHC, NHC),
                                             st2.ap(0, 128, NHC, NHC),
                                             AF.Sqrt, bias=EPS, scale=1.0),
                      waits=[P.cur("VE")])
            vop(lambda v: v.reciprocal(st2.ap(0, 128, NHC, NHC),
                                       st2.ap(0, 128, NHC, NHC)), waits=[s_j])
            vop(lambda v: v.tensor_tensor(st2.ap(0, 128, 3 * NHC, NHC),
                                          st2.ap(0, 128, 0, NHC),
                                          st2.ap(0, 128, NHC, NHC), ALU.mult))
            nb2 = vop(lambda v: v.tensor_scalar(st2.ap(0, 128, 3 * NHC, NHC),
                                                st2.ap(0, 128, 3 * NHC, NHC),
                                                -1.0, None, ALU.mult))
            for c in range(NHC):
                ob = c % 2
                on = sop(lambda s, c=c: s.activation(
                    x1bufs.ap(0, 128, c * D, D), x1bufs.ap(0, 128, c * D, D),
                    AF.Identity, bias=st2.h[0:128, 3 * NHC + c:3 * NHC + c + 1],
                    scale=st2.h[0:128, NHC + c:NHC + c + 1]),
                    waits=[nb2])
                sdma(yout[tok0 + c * 128:tok0 + c * 128 + 128, :],
                     x1bufs.ap(0, 128, c * D, D), waits=[on], sem=f"OD{ob}")
            bFFN = phase_barrier() + [P.cur("OD0"), P.cur("OD1")]
            for b in (st2, vxwf, vxwb, alscb[0], alscb[1], vxsb, hfm,
                      x2fm, scrb, scrf, stt, x1bufs):
                AR.freebuf(b, toks=bFFN)
            if half == 0:
                for b in (utb, v2tb, w2b, b1_sb):
                    AR.freebuf(b, toks=bFFN)
                h1_pre[0] = load_w("wkb2", wk, D, QD)
                h1_pre[1] = load_w("wvb2", wv, D, VD, col0=0, ncols=GV)

        P.wait("scalar", P.cur("OD0"))
        P.wait("scalar", P.cur("OD1"))
        P.wait("sync", P.cur("OD0"))
        P.wait("sync", P.cur("OD1"))

        with nc.Block() as block:
            @block.sync
            def _(eng):
                for fn in P.streams["sync"]:
                    fn(eng)

            @block.gpsimd
            def _(eng):
                for fn in P.streams["gpsimd"]:
                    fn(eng)

            @block.tensor
            def _(eng):
                for fn in P.streams["tensor"]:
                    fn(eng)

            @block.vector
            def _(eng):
                for fn in P.streams["vector"]:
                    fn(eng)

            @block.scalar
            def _(eng):
                for fn in P.streams["scalar"]:
                    fn(eng)

        build_stats = dict(ninst=dict(P.ninst), arena_peak=AR.peak)
        nc._mk_build_stats = build_stats
    return nc


def make_inputs(cfg, full, core):
    D, H, TP, TM = cfg["D"], cfg["H"], cfg["TP"], cfg["TM"]
    DK, DV, M, KE, R = cfg["DK"], cfg["DV"], cfg["M"], cfg["KE"], cfg["R"]
    KR = KE * R
    seq, halfid = core // 2, core % 2
    x = full["x"]
    xm = np.ascontiguousarray(x[seq, halfid * TM:(halfid + 1) * TM, :])
    if halfid == 1:
        xp = np.ascontiguousarray(x[seq, 0:TP, :])
        pb = np.zeros((128, 1), np.float32)
    else:
        xp = np.zeros((TP, D), np.float32)
        pb = np.full((128, 1), -1e38, np.float32)
    ones = np.ones
    f = np.float32
    return dict(
        x_main=xm, x_pre=xp,
        wq=full["wq"], wk=full["wk"], ws=full["ws"], wv=full["wv"],
        wg=full["wg"], wo=full["wo"], w1=full["w1"], w2=full["w2"],
        ut=full["ut"], v2t=full["v2t"],
        b1b=np.ascontiguousarray(np.broadcast_to(full["b1"], (128, D))).astype(f),
        b2b=np.ascontiguousarray(np.broadcast_to(full["b2"], (128, KE))).astype(f),
        gnwb=np.ascontiguousarray(np.broadcast_to(full["gnw"], (128, DV))).astype(f),
        csts=np.ascontiguousarray(np.broadcast_to(
            np.array([0.0, 1.0, EPS, 1.0 / DV, 0, 0, 0, 0], f), (128, 8))),
        pbias=pb,
        identf=np.eye(128, dtype=f),
        lmaskf=np.triu(np.ones((128, 128), f)),
        onescol=ones((128, 1), f),
        onesrow=ones((1, 128), f),
    )


def kernel(L0_attn_hidden_states, L0_attn_Wq, L0_attn_Wk, L0_attn_Wv,
           L0_attn_Wg, L0_attn_Ws, L0_attn_Wo, L0_attn_g_norm_weight,
           L0_ffn_U, L0_ffn_V, L0_ffn_W1, L0_ffn_b1, L0_ffn_W2, L0_ffn_b2):
    cfg = dict(DEF_CFG)
    D, KE, R = cfg["D"], cfg["KE"], cfg["R"]
    full = dict(
        x=np.ascontiguousarray(L0_attn_hidden_states),
        wq=np.ascontiguousarray(L0_attn_Wq), wk=np.ascontiguousarray(L0_attn_Wk),
        ws=np.ascontiguousarray(L0_attn_Ws), wv=np.ascontiguousarray(L0_attn_Wv),
        wg=np.ascontiguousarray(L0_attn_Wg), wo=np.ascontiguousarray(L0_attn_Wo),
        w1=np.ascontiguousarray(L0_ffn_W1), w2=np.ascontiguousarray(L0_ffn_W2),
        ut=np.ascontiguousarray(L0_ffn_U.transpose(0, 2, 1).reshape(KE * R, D)),
        v2t=np.ascontiguousarray(L0_ffn_V.transpose(2, 0, 1).reshape(D, KE * R)),
        b1=np.asarray(L0_ffn_b1), b2=np.asarray(L0_ffn_b2),
        gnw=np.asarray(L0_attn_g_norm_weight),
    )
    nc = build_program(cfg)
    in_maps = [make_inputs(cfg, full, c) for c in range(8)]
    res = run_bass_kernel_spmd(nc, in_maps, core_ids=list(range(8)))
    B, T = L0_attn_hidden_states.shape[0], L0_attn_hidden_states.shape[1]
    out = np.empty((B, T, D), np.float32)
    TM = cfg["TM"]
    for c in range(8):
        seq, halfid = c // 2, c % 2
        out[seq, halfid * TM:(halfid + 1) * TM, :] = res.results[c]["y"]
    return out
